# version 05
# baseline (speedup 1.0000x reference)
import math
import numpy as np
import ml_dtypes
import concourse.bass as bass
import concourse.mybir as mybir
from concourse.bass_utils import run_bass_kernel_spmd
from contextlib import ExitStack

F32 = mybir.dt.float32
BF16 = mybir.dt.bfloat16
U16 = mybir.dt.uint16
U32 = mybir.dt.uint32
U8 = mybir.dt.uint8
AF = mybir.ActivationFunctionType
OP = mybir.AluOpType
AX = mybir.AxisListType
bfdt = ml_dtypes.bfloat16

P = 128
LN_EPS = 1e-5


class Cfg:
    def __init__(self, T=2048, NEXP=65536, conv_chunks=64):
        self.T = T
        self.NT = T // 128
        self.NB = max(1, T // 512)
        self.TB = T // self.NB
        self.NTB = self.TB // 128
        self.NEXP = NEXP
        self.EW = 2048
        self.D = 1024
        self.DC = 8
        self.FH = 8
        self.NK = 256
        self.DK = 512
        self.DKC = 4
        self.PH = 16
        self.NP = 8
        self.conv_chunks = conv_chunks
        self.SLOTS = 128
        self.GW = 1
        self.GR = 6


class Emit:
    ENGS = ("sync", "scalar", "vector", "tensor", "gpsimd")

    def __init__(self, nc, block, ctx, sem_names):
        self.nc = nc
        self.block = block
        self.sems = {n: ctx.enter_context(nc.semaphore(n)) for n in sem_names}
        self.c = {n: 0 for n in sem_names}
        self.last_wait = {}
        self.pending = {e: [] for e in self.ENGS}

    def flush(self):
        for eng in self.ENGS:
            ops = self.pending[eng]
            if ops:
                getattr(self.block, eng)(lambda e, ops=ops: [f(e) for f in ops])
            self.pending[eng] = []

    CHAIN = {"vector": "dv", "scalar": "ac"}

    def op(self, eng, f, waits=(), inc=None, n=1):
        waits = list(waits)
        ch = self.CHAIN.get(eng)
        if ch is not None and f is not None:
            waits.append((ch, self.c[ch]))
        wl = []
        for s, v in waits:
            if v is None or v <= 0:
                continue
            k = (eng, s)
            if self.last_wait.get(k, -1) < v:
                wl.append((self.sems[s], v))
                self.last_wait[k] = v
        if inc is not None:
            sem = self.sems[inc]
            self.c[inc] += n
            def body(e, f=f, wl=wl, sem=sem, n=n):
                for s, v in wl:
                    e.wait_ge(s, v)
                f(e).then_inc(sem, n)
            self.pending[eng].append(body)
            return self.c[inc]
        else:
            def body(e, f=f, wl=wl):
                for s, v in wl:
                    e.wait_ge(s, v)
                if f is not None:
                    f(e)
            self.pending[eng].append(body)
            return None

    def fence(self):
        snap = [(s, v) for s, v in self.c.items() if v > 0]
        for eng in self.ENGS:
            self.op(eng, None, waits=snap)


def build_nc(cfg: Cfg):
    nc = bass.Bass("TRN2", target_bir_lowering=False)
    T, NT, NB, NTB, TB = cfg.T, cfg.NT, cfg.NB, cfg.NTB, cfg.TB
    D, DC, EW, FH, DKC, NK = cfg.D, cfg.DC, cfg.EW, cfg.FH, cfg.DKC, cfg.NK
    GW, GR, SLOTS = cfg.GW, cfg.GR, cfg.SLOTS
    NQB = T // 512

    z_in = nc.dram_tensor("z_in", [T, D], F32, kind="ExternalInput").ap()
    fused32 = nc.dram_tensor("fused32", [cfg.NEXP, EW], F32, kind="ExternalInput").ap()
    wqT_d = nc.dram_tensor("wqT", [D, 2 * FH * cfg.DK], F32, kind="ExternalInput").ap()
    keysT_d = nc.dram_tensor("keysT", [cfg.PH, cfg.DK, NK], F32, kind="ExternalInput").ap()
    wq_a = nc.dram_tensor("wq_a", [D, D], F32, kind="ExternalInput").ap()
    wk_a = nc.dram_tensor("wk_a", [D, D], F32, kind="ExternalInput").ap()
    wv_a = nc.dram_tensor("wv_a", [D, D], F32, kind="ExternalInput").ap()
    w_proj = nc.dram_tensor("w_proj", [D, D], F32, kind="ExternalInput").ap()
    rowv = nc.dram_tensor("rowv", [5, D], F32, kind="ExternalInput").ap()
    ident_in = nc.dram_tensor("ident_in", [P, P], F32, kind="ExternalInput").ap()

    dbg = {}
    if getattr(cfg, "debug", False):
        for nm, shp, dt in (("dbg_x1", [T, D], F32), ("dbg_x2", [T, D], F32),
                            ("dbg_eid", [P, NT, FH, 16], U32),
                            ("dbg_wgt", [P, NT, FH, 16], F32),
                            ("dbg_xn", [P, NT, D], BF16),
                            ("dbg_zn", [P, DC, T], BF16),
                            ("dbg_q", [P, T], BF16), ("dbg_k", [P, T], BF16),
                            ("dbg_o", [P, DC, T], BF16)):
            dbg[nm] = nc.dram_tensor(nm, shp, dt, kind="ExternalOutput").ap()
    fusedbf = nc.dram_tensor("fusedbf", [cfg.NEXP, EW], BF16, kind="Internal").ap()
    negm_dram = nc.dram_tensor("negm_dram", [2, T], BF16, kind="Internal").ap()
    x_out = nc.dram_tensor("x_out", [T, D], F32, kind="ExternalOutput").ap()

    ctx = ExitStack()
    with ctx:
        def sb(name, shape, dt):
            return ctx.enter_context(nc.sbuf_tensor(name, shape, dt))

        def psum(name, shape, dt=F32):
            return ctx.enter_context(nc.psum_tensor(name, shape, dt))

        X = sb("X", [P, NT, D], F32)
        gamma_b = sb("gamma_b", [P, D], F32)
        lns_b = sb("lns_b", [P, D], F32)
        lnb_b = sb("lnb_b", [P, D], F32)
        bproj_b = sb("bproj_b", [P, D], F32)
        ramp16 = sb("ramp16", [P, 16], F32)
        ident = sb("ident", [P, P], BF16)
        ones1 = sb("ones1", [1, P], BF16)
        eid = sb("eid", [P, NT, FH, 16], U32)
        wgt = sb("wgt", [P, NT, FH, 16], F32)
        junkb = sb("junkb", [P, 1024], BF16)
        stat = sb("stat", [P, 32], F32)
        kconst = sb("kconst", [P, 2], F32)

        ar1 = sb("ar1", [P, NT * D * 2], U8)
        xn_tok = ar1[:, :].bitcast(BF16).rearrange("p (a b) -> p a b", a=NT)
        zn_fm = ar1[:, :].bitcast(BF16).rearrange("p (a b) -> p a b", a=DC)

        AR2B = 75 * 1024
        ar2 = sb("ar2", [P, AR2B], U8)
        cursor = [0]

        def carve(shape, dt):
            esz = {BF16: 2, U16: 2, F32: 4, U32: 4, U8: 1}[dt]
            nb = int(np.prod(shape[1:])) * esz
            o = cursor[0]
            assert o + nb <= AR2B, f"arena2 overflow at {o + nb}"
            cursor[0] = (o + nb + 127) // 128 * 128
            v = ar2[:, o:o + nb].bitcast(dt)
            if len(shape) == 2:
                return v
            dims = dict(zip("ab", shape[1:-1]))
            pat = " ".join("ab"[: len(shape) - 2])
            return v.rearrange(f"p ({pat} z) -> p {pat} z", **dims)

        wqs = [carve([P, DC, 512], BF16) for _ in range(2)]
        keyss = [carve([P, DKC, NK], BF16) for _ in range(2)]
        xn_fm = carve([P, DC, TB], BF16)
        qT = carve([P, DKC, TB], BF16)
        sxsy = carve([P, NTB, 2 * FH * 16], F32)
        ixiy = carve([P, NTB, 2 * FH * 16], U16)
        scr256 = carve([P, 256], F32)
        grid = carve([P, 256], F32)
        gridm = carve([P, 256], F32)
        v16 = carve([P, 16], F32)
        p16 = carve([P, 16], U16)
        absel = carve([P, 32], U16)
        ohq = carve([P, 256], F32)
        usel = carve([P, 16], F32)
        wsel = carve([P, 16], F32)
        eidf = carve([P, 16], F32)
        ew16 = carve([P, 16], F32)
        ztmp = carve([P, D], F32)
        xupd = ztmp
        gbuf = carve([P, GR * GW, EW], BF16)
        hbuf = carve([P, SLOTS], F32)
        hgel = [carve([P, GW], F32) for _ in range(2)]
        hg = carve([P, SLOTS], F32)
        diag2 = [carve([P, GW * P], BF16) for _ in range(2)]
        ffn_end = cursor[0]

        cursor[0] = 0
        O_fm = carve([P, DC, T], BF16)
        az = carve([P, D], F32)
        aupd = carve([P, D], F32)
        attn_base = cursor[0]
        qTp = carve([P, T], BF16)
        kTp = carve([P, T], BF16)
        vP = carve([P, NT, 132], BF16)
        kaug = carve([P, T], BF16)
        qaug = carve([P, T], BF16)
        expT = carve([P, 4, 512], BF16)
        wstr = [carve([P, DC, P], BF16) for _ in range(2)]
        negm2 = carve([P, 2, NT], BF16)
        m4 = carve([P, 8], F32)
        zi_sb = carve([P, 512], BF16)
        zrowf = carve([P, 512], F32)
        zrowb = carve([P, 512], BF16)
        oev = carve([P, 512], BF16)
        attn_end = cursor[0]
        cursor[0] = attn_base
        pwb = [carve([P, DC, 512], BF16) for _ in range(2)]
        proj_end = cursor[0]
        assert max(ffn_end, attn_end, proj_end) <= AR2B

        ps_tr = psum("ps_tr", [P, 1024], BF16)
        ps_q = [psum(f"ps_q{i}", [P, 512], F32) for i in range(2)]
        ps_s = [psum(f"ps_s{i}", [P, 512], F32) for i in range(2)]
        ps_acc = psum("ps_acc", [P, 1024], F32)
        ps_x1 = psum("ps_x1", [P, 512], F32)

        sems = ["ld", "st", "ag", "od", "nm", "cvt", "sw",
                "sw0", "sw1", "gt0", "gt1", "gt2", "gt3", "gt4", "gt5",
                "pe", "dv", "ac", "pl"]
        block = ctx.enter_context(nc.Block())
        E = Emit(nc, block, ctx, sems)
        c = E.c
        st8 = {}

        E.op("sync", lambda e: e.dma_start(
            out=X[:], in_=z_in.rearrange("(t p) d -> p t d", p=P)), inc="ld", n=16)
        for i, buf in enumerate((gamma_b, lns_b, lnb_b, bproj_b)):
            E.op("sync", lambda e, i=i, buf=buf: e.dma_start(
                out=buf[:], in_=rowv[i:i + 1, :].to_broadcast([P, D])), inc="ld", n=16)
        E.op("sync", lambda e: e.dma_start(
            out=ramp16[:], in_=rowv[4:5, 0:16].to_broadcast([P, 16])), inc="ld", n=16)
        ld0 = c["ld"]
        E.op("gpsimd", lambda e: e.dma_start(out=ident[:], in_=ident_in), inc="sw", n=16)
        sw0_c = c["sw"]
        E.op("gpsimd", lambda e: e.memset(ones1[:], 1.0), inc="pl", n=1)
        pl_ones = c["pl"]
        E.op("vector", lambda e: e.memset(kconst[:, 0:1], LN_EPS), inc="dv")
        kc_dv = E.op("vector", lambda e: e.memset(kconst[:, 1:2], 0.125), inc="dv")
        rows_per = cfg.NEXP // cfg.conv_chunks
        cvt_done = 16 * cfg.conv_chunks
        conv_emitted = [0]

        def emit_conv_chunks(k):
            while conv_emitted[0] < cfg.conv_chunks and k > 0:
                ch = conv_emitted[0]
                E.op("gpsimd", lambda e, ch=ch: e.dma_start(
                    out=fusedbf[ch * rows_per:(ch + 1) * rows_per, :],
                    in_=fused32[ch * rows_per:(ch + 1) * rows_per, :]),
                    inc="cvt", n=16)
                conv_emitted[0] += 1
                k -= 1

        state = {
            "ln_src": {t: [("ld", ld0)] for t in range(NT)},
            "bank": {},
        }

        def bank_w(name):
            return state["bank"].get(name, [])

        def bank_r(name, sem, val):
            state["bank"][name] = [(sem, val)]

        def emit_ln(t, zdst, extra=()):
            w = list(state["ln_src"][t])
            E.op("vector", lambda e, t=t: e.bn_stats(
                out=stat[:, 0:6], in_=X[:, t, 0:512]), waits=w, inc="dv")
            E.op("vector", lambda e, t=t: e.bn_stats(
                out=stat[:, 6:12], in_=X[:, t, 512:1024]), inc="dv")
            d1 = E.op("vector", lambda e: e.bn_aggr(
                out=stat[:, 12:14], in_=stat[:, 0:12]),
                waits=[("dv", c["dv"])], inc="dv")
            a1 = E.op("scalar", lambda e: e.activation(
                out=stat[:, 14:15], in_=stat[:, 13:14], func=AF.Sqrt,
                bias=kconst[:, 0:1]),
                waits=[("dv", d1), ("dv", kc_dv)], inc="ac")
            d2 = E.op("vector", lambda e: e.reciprocal(
                out=stat[:, 15:16], in_=stat[:, 14:15]), waits=[("ac", a1)], inc="dv")
            d3 = E.op("vector", lambda e, t=t, zdst=zdst: e.tensor_scalar(
                out=zdst[:], in0=X[:, t, :], scalar1=stat[:, 12:13],
                scalar2=stat[:, 15:16], op0=OP.subtract, op1=OP.mult),
                waits=[("dv", d2)] + list(extra), inc="dv")
            return d3

        def emit_ffn(call):
            xn_done = {}
            for t in range(NT):
                dz = emit_ln(t, ztmp, extra=[("ac", state.get("ztmp_free", 0))])
                a1 = E.op("scalar", lambda e: e.activation(
                    out=junkb[:, 0:D], in_=ztmp[:], func=AF.Square,
                    accum_out=stat[:, 16:17]), waits=[("dv", dz)], inc="ac")
                state["ztmp_free"] = a1
                a2 = E.op("scalar", lambda e: e.activation(
                    out=stat[:, 17:18], in_=stat[:, 16:17], func=AF.Sqrt),
                    waits=[("ac", a1)], inc="ac")
                d1 = E.op("vector", lambda e: e.tensor_scalar(
                    out=stat[:, 18:19], in0=stat[:, 17:18], scalar1=1e-12,
                    scalar2=None, op0=OP.max, op1=OP.bypass),
                    waits=[("ac", a2)], inc="dv")
                d2 = E.op("vector", lambda e: e.reciprocal(
                    out=stat[:, 19:20], in_=stat[:, 18:19]), waits=[("dv", d1)], inc="dv")
                d3 = E.op("vector", lambda e: e.tensor_scalar(
                    out=stat[:, 20:21], in0=stat[:, 19:20],
                    scalar1=float(math.sqrt(D)), scalar2=None,
                    op0=OP.mult, op1=OP.bypass), waits=[("dv", d2)], inc="dv")
                d4 = E.op("vector", lambda e, t=t: e.scalar_tensor_tensor(
                    out=xn_tok[:, t, :], in0=ztmp[:], scalar=stat[:, 20:21],
                    in1=gamma_b[:], op0=OP.mult, op1=OP.mult),
                    waits=[("dv", d3)], inc="dv")
                xn_done[t] = d4

            def emit_bc(b):
                for tt in range(NTB):
                    t = b * NTB + tt
                    for cc in range(DC):
                        E.op("tensor", lambda e, t=t, cc=cc: e.transpose(
                            ps_tr[:, cc * 128:(cc + 1) * 128],
                            xn_tok[:, t, cc * 128:(cc + 1) * 128], ident[:]),
                            waits=[("dv", xn_done[t]), ("sw", sw0_c)] + bank_w("tr"),
                            inc="pe")
                    ev = E.op("vector", lambda e, tt=tt: e.tensor_copy(
                        out=xn_fm[:, :, tt * 128:(tt + 1) * 128],
                        in_=ps_tr[:].rearrange("p (a z) -> p a z", a=DC)),
                        waits=[("pe", c["pe"])], inc="dv")
                    bank_r("tr", "dv", ev)
                xn_fm_done = c["dv"]

                eid_done = {}
                for ph in range(cfg.PH):
                    par = ph % 2
                    gsem = "sw0" if par == 0 else "sw1"
                    E.op("gpsimd", lambda e, ph=ph, par=par: e.dma_start(
                        out=wqs[par][:, :, :],
                        in_=wqT_d[:, ph * 512:(ph + 1) * 512]
                            .rearrange("(a p) m -> p a m", p=P)),
                        waits=[("pe", state.get(("wqfree", par), 0))],
                        inc=gsem, n=16)
                    E.op("gpsimd", lambda e, ph=ph, par=par: e.dma_start(
                        out=keyss[par][:, :, :],
                        in_=keysT_d[ph].rearrange("(a p) m -> p a m", p=P)),
                        inc=gsem, n=16)
                    gval = c[gsem]
                    emit_conv_chunks(2)

                    qdone = [0] * DKC
                    for dkc in range(DKC):
                        qp = dkc % 2
                        for kc in range(DC):
                            E.op("tensor", lambda e, dkc=dkc, kc=kc, par=par, qp=qp:
                                 e.matmul(ps_q[qp][:, 0:TB],
                                          wqs[par][:, kc, dkc * 128:(dkc + 1) * 128],
                                          xn_fm[:, kc, :],
                                          start=(kc == 0), stop=(kc == DC - 1)),
                                 waits=[(gsem, gval), ("dv", xn_fm_done)]
                                 + bank_w(f"q{qp}"), inc="pe")
                        ev = E.op("scalar", lambda e, dkc=dkc, qp=qp: e.activation(
                            out=qT[:, dkc, :], in_=ps_q[qp][:, 0:TB], func=AF.Copy),
                            waits=[("pe", c["pe"])], inc="ac")
                        bank_r(f"q{qp}", "ac", ev)
                        qdone[dkc] = ev

                    p_, h_ = ph // FH, ph % FH
                    for tt in range(NTB):
                        sp = tt % 2
                        for dkc in range(DKC):
                            pe_l = E.op("tensor",
                                 lambda e, tt=tt, dkc=dkc, par=par, sp=sp:
                                 e.matmul(ps_s[sp][:, 0:NK],
                                          qT[:, dkc, tt * 128:(tt + 1) * 128],
                                          keyss[par][:, dkc, :],
                                          start=(dkc == 0), stop=(dkc == DKC - 1)),
                                 waits=[("ac", qdone[dkc]), (gsem, gval)]
                                 + bank_w(f"s{sp}"), inc="pe")
                        o1 = (p_ * FH + h_) * 16
                        d1 = E.op("vector", lambda e, tt=tt, sp=sp, o1=o1:
                             e.max(out=sxsy[:, tt, o1:o1 + 8], in_=ps_s[sp][:, 0:NK]),
                             waits=[("pe", c["pe"])], inc="dv")
                        d2 = E.op("vector", lambda e, tt=tt, sp=sp, o1=o1:
                             e.max_index(out=ixiy[:, tt, o1:o1 + 8],
                                         in_max=sxsy[:, tt, o1:o1 + 8],
                                         in_values=ps_s[sp][:, 0:NK]),
                             waits=[("dv", d1)], inc="dv")
                        d3 = E.op("vector", lambda e, tt=tt, sp=sp, o1=o1:
                             e.match_replace(out=scr256[:],
                                             in_to_replace=sxsy[:, tt, o1:o1 + 8],
                                             in_values=ps_s[sp][:, 0:NK],
                                             imm_value=-1e30),
                             waits=[("dv", d2)], inc="dv")
                        bank_r(f"s{sp}", "dv", d3)
                        d4 = E.op("vector", lambda e, tt=tt, o1=o1:
                             e.max(out=sxsy[:, tt, o1 + 8:o1 + 16], in_=scr256[:]),
                             waits=[("dv", d3)], inc="dv")
                        E.op("vector", lambda e, tt=tt, o1=o1:
                             e.max_index(out=ixiy[:, tt, o1 + 8:o1 + 16],
                                         in_max=sxsy[:, tt, o1 + 8:o1 + 16],
                                         in_values=scr256[:]),
                             waits=[("dv", d4)], inc="dv")
                    state[("wqfree", par)] = c["pe"]

                stage1_done = c["dv"]
                for tt in range(NTB):
                    t = b * NTB + tt
                    for h_ in range(FH):
                        ox = (0 * FH + h_) * 16
                        oy = (1 * FH + h_) * 16
                        g1 = E.op("vector", lambda e, tt=tt, ox=ox, oy=oy:
                             e.tensor_tensor(
                                 out=grid[:],
                                 in0=sxsy[:, tt, ox:ox + 16].unsqueeze(2)
                                     .to_broadcast([P, 16, 16]),
                                 in1=sxsy[:, tt, oy:oy + 16].unsqueeze(1)
                                     .to_broadcast([P, 16, 16]),
                                 op=OP.add),
                             waits=[("dv", stage1_done)], inc="dv")
                        g2 = E.op("vector", lambda e: e.max(
                            out=v16[:, 0:8], in_=grid[:]), waits=[("dv", g1)], inc="dv")
                        g3 = E.op("vector", lambda e: e.max_index(
                            out=p16[:, 0:8], in_max=v16[:, 0:8], in_values=grid[:]),
                            waits=[("dv", g2)], inc="dv")
                        g4 = E.op("vector", lambda e: e.match_replace(
                            out=gridm[:], in_to_replace=v16[:, 0:8],
                            in_values=grid[:], imm_value=-1e30),
                            waits=[("dv", g3)], inc="dv")
                        g5 = E.op("vector", lambda e: e.max(
                            out=v16[:, 8:16], in_=gridm[:]), waits=[("dv", g4)], inc="dv")
                        g6 = E.op("vector", lambda e: e.max_index(
                            out=p16[:, 8:16], in_max=v16[:, 8:16], in_values=gridm[:]),
                            waits=[("dv", g5)], inc="dv")
                        g7 = E.op("vector", lambda e: e.tensor_scalar(
                            out=absel[:, 0:16], in0=p16[:], scalar1=4, scalar2=None,
                            op0=OP.logical_shift_right, op1=OP.bypass),
                            waits=[("dv", g6)], inc="dv")
                        g8 = E.op("vector", lambda e: e.tensor_scalar(
                            out=absel[:, 16:32], in0=p16[:], scalar1=15, scalar2=None,
                            op0=OP.bitwise_and, op1=OP.bypass), inc="dv")
                        g9 = E.op("vector", lambda e: e.tensor_tensor(
                            out=ohq[:],
                            in0=absel[:, 0:16].unsqueeze(2).to_broadcast([P, 16, 16]),
                            in1=ramp16[:].unsqueeze(1).to_broadcast([P, 16, 16]),
                            op=OP.is_equal), waits=[("dv", g7)], inc="dv")
                        g10 = E.op("vector", lambda e, tt=tt, ox=ox: e.tensor_tensor(
                            out=ohq[:], in0=ohq[:],
                            in1=ixiy[:, tt, ox:ox + 16].unsqueeze(1)
                                .to_broadcast([P, 16, 16]),
                            op=OP.mult), waits=[("dv", g9)], inc="dv")
                        g11 = E.op("vector", lambda e: e.tensor_reduce(
                            out=usel[:], in_=ohq[:].rearrange("p (a z) -> p a z", a=16),
                            axis=AX.X, op=OP.add), waits=[("dv", g10)], inc="dv")
                        g12 = E.op("vector", lambda e: e.tensor_tensor(
                            out=ohq[:],
                            in0=absel[:, 16:32].unsqueeze(2).to_broadcast([P, 16, 16]),
                            in1=ramp16[:].unsqueeze(1).to_broadcast([P, 16, 16]),
                            op=OP.is_equal),
                            waits=[("dv", g11), ("dv", g8)], inc="dv")
                        g13 = E.op("vector", lambda e, tt=tt, oy=oy: e.tensor_tensor(
                            out=ohq[:], in0=ohq[:],
                            in1=ixiy[:, tt, oy:oy + 16].unsqueeze(1)
                                .to_broadcast([P, 16, 16]),
                            op=OP.mult), waits=[("dv", g12)], inc="dv")
                        g14 = E.op("vector", lambda e: e.tensor_reduce(
                            out=wsel[:], in_=ohq[:].rearrange("p (a z) -> p a z", a=16),
                            axis=AX.X, op=OP.add), waits=[("dv", g13)], inc="dv")
                        g15 = E.op("vector", lambda e: e.scalar_tensor_tensor(
                            out=eidf[:], in0=usel[:], scalar=256.0, in1=wsel[:],
                            op0=OP.mult, op1=OP.add), waits=[("dv", g14)], inc="dv")
                        g16 = E.op("vector", lambda e, t=t, h_=h_: e.tensor_copy(
                            out=eid[:, t, h_, :], in_=eidf[:]),
                            waits=[("dv", g15)] + [
                                (f"gt{r}", st8.get(("eidfree", t, h_, r), 0))
                                for r in range(GR)],
                            inc="dv")
                        g17 = E.op("vector", lambda e: e.tensor_scalar(
                            out=stat[:, 21:22], in0=v16[:, 0:1], scalar1=-1.0,
                            scalar2=None, op0=OP.mult, op1=OP.bypass),
                            waits=[("dv", g6)], inc="dv")
                        a1 = E.op("scalar", lambda e: e.activation(
                            out=ew16[:], in_=v16[:], func=AF.Exp, bias=stat[:, 21:22],
                            accum_out=stat[:, 22:23]), waits=[("dv", g17)], inc="ac")
                        dz = E.op("vector", lambda e: e.reciprocal(
                            out=stat[:, 23:24], in_=stat[:, 22:23]),
                            waits=[("ac", a1)], inc="dv")
                        g18 = E.op("vector", lambda e, t=t, h_=h_: e.tensor_scalar(
                            out=wgt[:, t, h_, :], in0=ew16[:],
                            scalar1=stat[:, 23:24], scalar2=None,
                            op0=OP.mult, op1=OP.bypass), waits=[("dv", dz)], inc="dv")
                        eid_done[(t, h_)] = max(g16, g18)
                return eid_done

            def emit_d(b, eid_done):
                for tt in range(NTB):
                    t = b * NTB + tt
                    edone = max(eid_done[(t, h_)] for h_ in range(FH))
                    nw = SLOTS // GW
                    wflat = wgt[:, t, :, :].rearrange("p a z -> p (a z)")

                    def emit_tail(wv, t=t, wflat=wflat, edone=edone):
                        pp = wv % 2
                        s0 = wv * GW
                        ring = wv % GR
                        a1 = state[("gelu", wv % 4)]
                        d5 = E.op("vector", lambda e, s0=s0, wflat=wflat, pp=pp:
                             e.tensor_tensor(
                                 out=hg[:, s0:s0 + GW], in0=hgel[pp][:],
                                 in1=wflat[:, s0:s0 + GW], op=OP.mult),
                             waits=[("ac", a1), ("dv", edone)], inc="dv")
                        if GW == 1:
                            d6 = E.op("scalar", lambda e, s0=s0, pp=pp:
                                 e.activation(
                                     out=diag2[pp][:], in_=ident[:], func=AF.Copy,
                                     scale=hg[:, s0:s0 + 1]),
                                 waits=[("dv", d5),
                                        ("pe", state.get(("diagfree", pp), 0))],
                                 inc="ac")
                            d6w = ("ac", d6)
                        else:
                            d6 = E.op("vector", lambda e, s0=s0, pp=pp:
                                 e.scalar_tensor_tensor(
                                     out=diag2[pp][:],
                                     in0=ident[:].unsqueeze(1).to_broadcast([P, GW, P]),
                                     scalar=1.0,
                                     in1=hg[:, s0:s0 + GW].unsqueeze(2)
                                         .to_broadcast([P, GW, P]),
                                     op0=OP.mult, op1=OP.mult),
                                 waits=[("dv", d5),
                                        ("pe", state.get(("diagfree", pp), 0))],
                                 inc="dv")
                            d6w = ("dv", d6)
                        for j in range(GW):
                            s = wv * GW + j
                            for half in range(2):
                                E.op("tensor",
                                     lambda e, ring=ring, j=j, s=s, half=half, pp=pp:
                                     e.matmul(ps_acc[:, half * 512:(half + 1) * 512],
                                              diag2[pp][:, j * P:(j + 1) * P],
                                              gbuf[:, ring * GW + j,
                                                   D + half * 512:D + (half + 1) * 512],
                                              start=(s == 0), stop=(s == SLOTS - 1),
                                              skip_group_check=True),
                                     waits=[d6w] +
                                     (bank_w("acc") if s == 0 and half == 0 else []),
                                     inc="pe")
                        state[("gslot", ring)] = (state[("dots", ring)], c["pe"])
                        state[("diagfree", pp)] = c["pe"]

                    for wv in range(nw):
                        ring = wv % GR
                        gsem = f"gt{ring}"
                        war = state.get(("gslot", ring))
                        waits = [("cvt", cvt_done), ("dv", edone)]
                        if war:
                            waits += [("dv", war[0]), ("pe", war[1])]
                        for j in range(GW):
                            s = wv * GW + j
                            h_, k_ = s // 16, s % 16
                            E.op("gpsimd", lambda e, t=t, h_=h_, k_=k_, ring=ring, j=j:
                                 e.indirect_dma_start(
                                     out=gbuf[:, ring * GW + j, :], out_offset=None,
                                     in_=fusedbf,
                                     in_offset=bass.IndirectOffsetOnAxis(
                                         ap=eid[:, t, h_, k_:k_ + 1], axis=0)),
                                 waits=waits, inc=gsem, n=16)
                            waits = []
                            st8[("eidfree", t, h_, ring)] = c[gsem]
                        gval = c[gsem]
                        dots = 0
                        for j in range(GW):
                            s = wv * GW + j
                            dots = E.op("vector", lambda e, t=t, ring=ring, j=j, s=s:
                                 e.scalar_tensor_tensor(
                                     out=junkb[:, 0:D],
                                     in0=gbuf[:, ring * GW + j, 0:D],
                                     scalar=1.0, in1=xn_tok[:, t, :],
                                     op0=OP.mult, op1=OP.mult,
                                     accum_out=hbuf[:, s:s + 1]),
                                 waits=[(gsem, gval), ("dv", xn_done[t]),
                                        ("ac", state.get(("hfree", wv), 0))],
                                 inc="dv")
                        state[("dots", ring)] = dots
                        s0 = wv * GW
                        a1 = E.op("scalar", lambda e, s0=s0, pp=wv % 2: e.activation(
                            out=hgel[pp][:], in_=hbuf[:, s0:s0 + GW], func=AF.Gelu),
                            waits=[("dv", dots),
                                   ("dv", state.get(("hgelfree", wv % 2), 0))],
                            inc="ac")
                        state[("hfree", wv)] = a1
                        state[("gelu", wv % 4)] = a1
                        if wv > 0:
                            emit_tail(wv - 1)
                            state[("hgelfree", (wv - 1) % 2)] = c["dv"]
                    emit_tail(nw - 1)
                    up_done = c["pe"]
                    d7 = E.op("vector", lambda e, t=t: e.scalar_tensor_tensor(
                        out=xupd[:], in0=ps_acc[:], scalar=0.5, in1=X[:, t, :],
                        op0=OP.mult, op1=OP.add), waits=[("pe", up_done)], inc="dv")
                    bank_r("acc", "dv", d7)
                    d8 = E.op("vector", lambda e, t=t: e.tensor_copy(
                        out=X[:, t, :], in_=xupd[:]), waits=[("dv", d7)], inc="dv")
                    state["ln_src"][t] = [("dv", d8)]

            ed = [None] * NB
            if call == 0:
                for b in range(NB):
                    ed[b] = emit_bc(b)
                for b in range(NB):
                    emit_d(b, ed[b])
            else:
                ed[0] = emit_bc(0)
                for b in range(NB):
                    if b + 1 < NB:
                        ed[b + 1] = emit_bc(b + 1)
                    emit_d(b, ed[b])

        def emit_attn():
            zn_done = 0
            for t in range(NT):
                dz = emit_ln(t, az)
                d1 = E.op("vector", lambda e, t=t: e.tensor_tensor(
                    out=aupd[:], in0=X[:, t, :], in1=az[:], op=OP.add),
                    waits=[("dv", dz)], inc="dv")
                d2 = E.op("vector", lambda e, t=t: e.tensor_copy(
                    out=X[:, t, :], in_=aupd[:]), waits=[("dv", d1)], inc="dv")
                state["ln_src"][t] = [("dv", d2)]
                E.op("vector", lambda e: e.bn_stats(out=stat[:, 0:6], in_=az[:, 0:512]),
                     waits=[("dv", dz)], inc="dv")
                E.op("vector", lambda e: e.bn_stats(out=stat[:, 6:12], in_=az[:, 512:1024]),
                     inc="dv")
                d3 = E.op("vector", lambda e: e.bn_aggr(
                    out=stat[:, 12:14], in_=stat[:, 0:12]),
                    waits=[("dv", c["dv"])], inc="dv")
                a1 = E.op("scalar", lambda e: e.activation(
                    out=stat[:, 14:15], in_=stat[:, 13:14], func=AF.Sqrt,
                    bias=kconst[:, 0:1]),
                    waits=[("dv", d3), ("dv", kc_dv)], inc="ac")
                d4 = E.op("vector", lambda e: e.reciprocal(
                    out=stat[:, 15:16], in_=stat[:, 14:15]), waits=[("ac", a1)], inc="dv")
                d5 = E.op("vector", lambda e: e.tensor_scalar(
                    out=aupd[:], in0=az[:], scalar1=stat[:, 12:13],
                    scalar2=stat[:, 15:16], op0=OP.subtract, op1=OP.mult),
                    waits=[("dv", d4), ("dv", d2)], inc="dv")
                d6 = E.op("vector", lambda e: e.tensor_tensor(
                    out=aupd[:], in0=aupd[:], in1=lns_b[:], op=OP.mult),
                    waits=[("dv", d5)], inc="dv")
                d7 = E.op("vector", lambda e: e.tensor_tensor(
                    out=junkb[:, 0:D], in0=aupd[:], in1=lnb_b[:], op=OP.add),
                    waits=[("dv", d6), ("pe", state.get("znj_free", 0))], inc="dv")
                for cc in range(DC):
                    E.op("tensor", lambda e, cc=cc: e.transpose(
                        ps_tr[:, cc * 128:(cc + 1) * 128],
                        junkb[:, cc * 128:(cc + 1) * 128], ident[:]),
                        waits=[("dv", d7), ("sw", sw0_c)] + bank_w("tr"), inc="pe")
                state["znj_free"] = c["pe"]
                ev = E.op("vector", lambda e, t=t: e.tensor_copy(
                    out=zn_fm[:, :, t * 128:(t + 1) * 128],
                    in_=ps_tr[:].rearrange("p (a z) -> p a z", a=DC)),
                    waits=[("pe", c["pe"])], inc="dv")
                bank_r("tr", "dv", ev)
                zn_done = ev

            for hp in range(cfg.NP):
                par = hp % 2
                gsem = "sw0" if par == 0 else "sw1"
                qk_ev = {}
                for which, (wt, buf, scl) in enumerate(
                        ((wq_a, qTp, "q"), (wk_a, kTp, None))):
                    E.op("gpsimd", lambda e, wt=wt, hp=hp, par=par: e.dma_start(
                        out=wstr[par][:, :, :],
                        in_=wt[:, hp * 128:(hp + 1) * 128]
                            .rearrange("(a p) m -> p a m", p=P)),
                        waits=[("pe", c["pe"])], inc=gsem, n=16)
                    gval = c[gsem]
                    for tb in range(NQB):
                        qp = tb % 2
                        for kc in range(DC):
                            E.op("tensor", lambda e, tb=tb, kc=kc, par=par, qp=qp:
                                 e.matmul(ps_q[qp][:],
                                          wstr[par][:, kc, :],
                                          zn_fm[:, kc, tb * 512:(tb + 1) * 512],
                                          start=(kc == 0), stop=(kc == DC - 1)),
                                 waits=[(gsem, gval), ("dv", zn_done)]
                                 + bank_w(f"q{qp}"), inc="pe")
                        if scl == "q":
                            ev = E.op("scalar", lambda e, tb=tb, buf=buf, qp=qp:
                                      e.activation(out=buf[:, tb * 512:(tb + 1) * 512],
                                                   in_=ps_q[qp][:], func=AF.Copy,
                                                   scale=kconst[:, 1:2]),
                                      waits=[("pe", c["pe"]), ("dv", kc_dv)], inc="ac")
                        else:
                            ev = E.op("scalar", lambda e, tb=tb, buf=buf, qp=qp:
                                      e.activation(out=buf[:, tb * 512:(tb + 1) * 512],
                                                   in_=ps_q[qp][:], func=AF.Copy),
                                      waits=[("pe", c["pe"])], inc="ac")
                        bank_r(f"q{qp}", "ac", ev)
                    state[("wsfree", par)] = c["pe"]
                    qk_ev[which] = c["ac"]
                E.op("gpsimd", lambda e, hp=hp, par=par: e.dma_start(
                    out=wstr[par][:, :, :],
                    in_=wv_a[:, hp * 128:(hp + 1) * 128]
                        .rearrange("(a p) m -> p a m", p=P)),
                    waits=[("pe", c["pe"])], inc=gsem, n=16)
                gval = c[gsem]
                dvo = E.op("vector", lambda e: e.memset(vP[:, :, 64:65], 1.0),
                           waits=[("pe", state.get("pvfree", 0))], inc="dv")
                dvo = E.op("vector", lambda e: e.memset(vP[:, :, 130:131], 1.0),
                           inc="dv")
                for t in range(NT):
                    qp = t % 2
                    for kc in range(DC):
                        E.op("tensor", lambda e, t=t, kc=kc, par=par, qp=qp:
                             e.matmul(ps_q[qp][:, 0:128],
                                      zn_fm[:, kc, t * 128:(t + 1) * 128],
                                      wstr[par][:, kc, :],
                                      start=(kc == 0), stop=(kc == DC - 1)),
                             waits=[(gsem, gval), ("dv", zn_done), ("dv", dvo)]
                             + bank_w(f"q{qp}"), inc="pe")
                    e1 = E.op("vector", lambda e, t=t, qp=qp: e.tensor_copy(
                        out=vP[:, t, 0:64], in_=ps_q[qp][:, 0:64]),
                        waits=[("pe", c["pe"])], inc="dv")
                    e2 = E.op("vector", lambda e, t=t, qp=qp: e.tensor_copy(
                        out=vP[:, t, 66:130], in_=ps_q[qp][:, 64:128]), inc="dv")
                    bank_r(f"q{qp}", "dv", e2)
                v_done = c["dv"]
                state[("wsfree", par)] = c["pe"]

                agw = [("ac", qk_ev[0]), ("ac", qk_ev[1]),
                       ("pe", state.get("augfree", 0))]
                E.op("sync", lambda e: e.dma_start(
                    out=qaug[0:64, :], in_=qTp[0:64, :]), waits=agw, inc="ag", n=16)
                E.op("sync", lambda e: e.dma_start(
                    out=kaug[0:64, :], in_=kTp[0:64, :]), inc="ag", n=16)
                aug_ld = c["ag"]
                ko = E.op("vector", lambda e: e.memset(kaug[64:65, :], 1.0),
                          waits=[("ag", aug_ld), ("pe", state.get("augfree", 0))],
                          inc="dv")

                sbanks = {0: ("q0", ps_q[0]), 1: ("q1", ps_q[1])}
                sbanks2 = {0: ("s0", ps_s[0]), 1: ("s1", ps_s[1])}
                for t in range(NT):
                    rm_last = {}
                    for tkb in range(NQB):
                        for side in range(2):
                            bname, pss = (sbanks if tkb % 2 == 0 else sbanks2)[side]
                            pv = E.op("tensor",
                                 lambda e, t=t, tkb=tkb, side=side, pss=pss:
                                 e.matmul(pss[:],
                                          qTp[side * 64:(side + 1) * 64,
                                              t * 128:(t + 1) * 128],
                                          kTp[side * 64:(side + 1) * 64,
                                              tkb * 512:(tkb + 1) * 512],
                                          start=True, stop=True,
                                          tile_position=(side * 64, 0)),
                                 waits=[("ac", qk_ev[1]), ("ag", aug_ld)]
                                 + bank_w(bname), inc="pe")
                            rm = E.op("vector",
                                 lambda e, t=t, tkb=tkb, side=side, pss=pss:
                                 e.tensor_scalar(
                                     out=junkb[:, 0:512], in0=pss[:],
                                     scalar1=1.0, scalar2=-1e30,
                                     op0=OP.mult, op1=OP.max,
                                     accum_out=m4[:, side * 4 + tkb:
                                                  side * 4 + tkb + 1]),
                                 waits=[("pe", pv)], inc="dv")
                            bank_r(bname, "dv", rm)
                            rm_last[side] = rm
                    for side in range(2):
                        dmx = E.op("vector", lambda e, side=side: e.tensor_reduce(
                            out=stat[:, 24:25],
                            in_=m4[:, side * 4:side * 4 + NQB],
                            axis=AX.X, op=OP.max),
                            waits=[("dv", rm_last[side])], inc="dv")
                        E.op("vector", lambda e, t=t, side=side: e.tensor_scalar(
                            out=negm2[:, side, t:t + 1], in0=stat[:, 24:25],
                            scalar1=-1.0, scalar2=None, op0=OP.mult, op1=OP.bypass),
                            waits=[("dv", dmx), ("nm", state.get("negm_free", 0))],
                            inc="dv")
                negm_dv = c["dv"]
                for side in range(2):
                    E.op("sync", lambda e, side=side: e.dma_start(
                        out=negm_dram[side].rearrange("(t p) -> p t", p=P),
                        in_=negm2[:, side, :],
                        allow_slow_non_contiguous=True),
                        waits=[("dv", negm_dv)], inc="nm", n=16)
                nm_st = c["nm"]
                state["negm_free"] = nm_st

                for side in range(2):
                    if side == 1:
                        agw2 = [("pe", c["pe"])]
                        E.op("sync", lambda e: e.dma_start(
                            out=qaug[0:64, :], in_=qTp[64:128, :]),
                            waits=agw2, inc="ag", n=16)
                        E.op("sync", lambda e: e.dma_start(
                            out=kaug[0:64, :], in_=kTp[64:128, :]), inc="ag", n=16)
                        aug_ld2 = c["ag"]
                        ko = E.op("vector", lambda e: e.memset(kaug[64:65, :], 1.0),
                                  waits=[("ag", aug_ld2)], inc="dv")
                    nrow = E.op("sync", lambda e, side=side: e.dma_start(
                        out=qaug[64:65, :], in_=negm_dram[side:side + 1, :]),
                        waits=[("nm", nm_st),
                               ("pe", c["pe"] if side == 1
                                else state.get("augfree", 0)),
                               ("ag", c["ag"])], inc="nm", n=16)
                    nm_row = c["nm"]
                    vofs = 0 if side == 0 else 66
                    h = 2 * hp + side
                    chunk, half = hp, side
                    for qb in range(NQB):
                        aes = {}

                        def emit_pv(tkt, vofs=vofs):
                            er = tkt % 4
                            pv2 = E.op("tensor", lambda e, tkt=tkt, er=er, vofs=vofs:
                                 e.matmul(ps_x1[0:65, :],
                                          vP[:, tkt, vofs:vofs + 65],
                                          expT[:, er, :],
                                          start=(tkt == 0), stop=(tkt == NT - 1),
                                          skip_group_check=True),
                                 waits=[("ac", aes[tkt]), ("dv", v_done)]
                                 + (bank_w("x1") if tkt == 0 else []), inc="pe")
                            state[("pvc", er)] = pv2
                            return pv2

                        pvdone = 0
                        for tkt in range(NT):
                            er = tkt % 4
                            stp = tkt % 2
                            pv = E.op("tensor", lambda e, tkt=tkt, qb=qb, stp=stp:
                                 e.matmul(ps_acc[:, stp * 512:(stp + 1) * 512],
                                          kaug[0:65, tkt * 128:(tkt + 1) * 128],
                                          qaug[0:65, qb * 512:(qb + 1) * 512],
                                          start=True, stop=True),
                                 waits=[("nm", nm_row), ("dv", ko)]
                                 + bank_w(f"a{stp}"), inc="pe")
                            ae = E.op("scalar", lambda e, er=er, stp=stp:
                                 e.activation(out=expT[:, er, :],
                                              in_=ps_acc[:, stp * 512:(stp + 1) * 512],
                                              func=AF.Exp),
                                 waits=[("pe", pv), ("pe", state.get(("pvc", er), 0))],
                                 inc="ac")
                            bank_r(f"a{stp}", "ac", ae)
                            aes[tkt] = ae
                            if tkt >= 1:
                                pvdone = emit_pv(tkt - 1)
                        pvdone = emit_pv(NT - 1)
                        r1 = E.op("vector", lambda e: e.reciprocal(
                            out=zrowf[0:1, :], in_=ps_x1[64:65, :]),
                            waits=[("pe", pvdone)], inc="dv")
                        r2 = E.op("vector", lambda e: e.tensor_copy(
                            out=zrowb[0:1, :], in_=zrowf[0:1, :]),
                            waits=[("dv", r1)], inc="dv")
                        bz = E.op("tensor", lambda e: e.matmul(
                            ps_q[0][0:64, :], ones1[:, 0:64], zrowb[0:1, :],
                            start=True, stop=True),
                            waits=[("dv", r2), ("pl", pl_ones)]
                            + bank_w("q0"), inc="pe")
                        ze = E.op("vector", lambda e: e.tensor_copy(
                            out=zi_sb[0:64, :], in_=ps_q[0][0:64, :]),
                            waits=[("pe", bz)], inc="dv")
                        bank_r("q0", "dv", ze)
                        oe = E.op("vector", lambda e: e.tensor_tensor(
                            out=oev[0:64, :], in0=ps_x1[0:64, :],
                            in1=zi_sb[0:64, :], op=OP.mult),
                            waits=[("dv", ze), ("od", state.get("oev_free", 0))],
                            inc="dv")
                        bank_r("x1", "dv", oe)
                        E.op("sync", lambda e, qb=qb, chunk=chunk, half=half:
                             e.dma_start(
                                 out=O_fm[half * 64:(half + 1) * 64, chunk,
                                          qb * 512:(qb + 1) * 512],
                                 in_=oev[0:64, :]),
                             waits=[("dv", oe)], inc="od", n=16)
                        state["oev_free"] = c["od"]
                state["augfree"] = c["pe"]
                state["pvfree"] = c["pe"]
            o_all = c["od"]

            for half in range(2):
                E.op("gpsimd", lambda e, half=half: e.dma_start(
                    out=pwb[half][:, :, :],
                    in_=w_proj[:, half * 512:(half + 1) * 512]
                        .rearrange("(a p) m -> p a m", p=P)),
                    waits=[("pe", c["pe"]), ("dv", c["dv"]), ("od", c["od"]),
                           ("ac", c["ac"]), ("nm", c["nm"]), ("ag", c["ag"])],
                    inc="sw", n=16)
            pw_ld = c["sw"]
            for t in range(NT):
                xw = []
                for half in range(2):
                    hp_ = half
                    bname = "q0" if half == 0 else "q1"
                    pss = ps_q[half]
                    for kc in range(DC):
                        E.op("tensor", lambda e, t=t, kc=kc, half=half, pss=pss:
                             e.matmul(pss[:],
                                      O_fm[:, kc, t * 128:(t + 1) * 128],
                                      pwb[half][:, kc, :],
                                      start=(kc == 0), stop=(kc == DC - 1)),
                             waits=[("sw", pw_ld), ("od", o_all)]
                             + bank_w(bname), inc="pe")
                    e1 = E.op("vector", lambda e, t=t, half=half, pss=pss:
                         e.scalar_tensor_tensor(
                             out=aupd[:, half * 512:(half + 1) * 512],
                             in0=pss[:], scalar=1.0,
                             in1=bproj_b[:, half * 512:(half + 1) * 512],
                             op0=OP.mult, op1=OP.add),
                         waits=[("pe", c["pe"])], inc="dv")
                    bank_r(bname, "dv", e1)
                    e2 = E.op("vector", lambda e, t=t, half=half: e.tensor_tensor(
                        out=az[:, half * 512:(half + 1) * 512],
                        in0=aupd[:, half * 512:(half + 1) * 512],
                        in1=X[:, t, half * 512:(half + 1) * 512], op=OP.add),
                        waits=[("dv", e1)] + state["ln_src"][t], inc="dv")
                    xw.append(e2)
                e3 = E.op("vector", lambda e, t=t: e.tensor_copy(
                    out=X[:, t, :], in_=az[:]),
                    waits=[("dv", max(xw))], inc="dv")
                state["ln_src"][t] = [("dv", e3)]

        phases = getattr(cfg, "phases", "faf")
        def dump(name, src):
            if name in dbg:
                tgt = dbg[name]
                if name in ("dbg_x1", "dbg_x2"):
                    tgt = tgt.rearrange("(t p) d -> p t d", p=P)
                E.op("sync", lambda e, tgt=tgt, src=src: e.dma_start(
                    out=tgt, in_=src),
                    waits=[(k, v) for k, v in c.items() if v > 0], inc="st", n=16)

        emit_ffn(0)
        E.fence()
        dump("dbg_x1", X[:].rearrange("p t d -> t p d") if False else X[:])
        dump("dbg_eid", eid[:])
        dump("dbg_wgt", wgt[:])
        dump("dbg_xn", xn_tok[:, :, :])
        E.fence()
        if len(phases) > 1:
            emit_attn()
            E.fence()
            dump("dbg_x2", X[:])
            dump("dbg_zn", zn_fm[:, :, :])
            dump("dbg_o", O_fm[:, :, :])
            dump("dbg_q", qTp[:])
            dump("dbg_k", kTp[:])
            E.fence()
        if len(phases) > 2:
            emit_ffn(1)

        stw = [w for t in range(NT) for w in state["ln_src"][t]]
        E.op("sync", lambda e: e.dma_start(
            out=x_out.rearrange("(t p) d -> p t d", p=P), in_=X[:]),
            waits=stw, inc="st", n=16)
        E.op("sync", None, waits=[("st", c["st"])])
        E.flush()

    return nc



def _host_prep(inputs, cfg):
    D = cfg.D
    w_qkv = np.asarray(inputs["L0_attn_w_qkv"], dtype=np.float32)
    H, HD = 16, 64
    cols = np.arange(D)
    h, hd = cols // HD, cols % HD
    wq_a = np.ascontiguousarray(w_qkv[:, h * (HD * 3) + hd * 3 + 0])
    wk_a = np.ascontiguousarray(w_qkv[:, h * (HD * 3) + hd * 3 + 1])
    wv_a = np.ascontiguousarray(w_qkv[:, h * (HD * 3) + hd * 3 + 2])
    wqT = np.ascontiguousarray(np.asarray(inputs["L0_ffn_wq"], np.float32).T)
    keys = np.asarray(inputs["L0_ffn_keys"], np.float32)
    keysT = np.ascontiguousarray(
        keys.transpose(2, 0, 3, 1).reshape(cfg.PH, cfg.DK, cfg.NK))
    fused32 = np.ascontiguousarray(np.concatenate(
        [np.asarray(inputs["L0_ffn_table_down"], np.float32),
         np.asarray(inputs["L0_ffn_table_up"], np.float32)], axis=1))
    rowv = np.zeros((5, D), np.float32)
    rowv[0] = np.asarray(inputs["L0_ffn_gamma"], np.float32)
    rowv[1] = np.asarray(inputs["L0_attn_ln_scale"], np.float32)
    rowv[2] = np.asarray(inputs["L0_attn_ln_bias"], np.float32)
    rowv[3] = np.asarray(inputs["L0_attn_b_proj"], np.float32)
    rowv[4, 0:16] = np.arange(16, dtype=np.float32)
    ident = np.eye(P, dtype=np.float32)
    w_proj = np.ascontiguousarray(np.asarray(inputs["L0_attn_w_proj"], np.float32))
    shared = dict(fused32=fused32, wqT=wqT, keysT=keysT, wq_a=wq_a, wk_a=wk_a,
                  wv_a=wv_a, w_proj=w_proj, rowv=rowv, ident_in=ident)
    return shared


def kernel(**inputs):
    cfg = Cfg()
    z = np.asarray(inputs["L0_attn_z"], np.float32)
    ncore = z.shape[0]
    shared = _host_prep(inputs, cfg)
    in_maps = []
    for cidx in range(ncore):
        m = dict(shared)
        m["z_in"] = np.ascontiguousarray(z[cidx])
        in_maps.append(m)
    nc = build_nc(cfg)
    res = run_bass_kernel_spmd(nc, in_maps, core_ids=list(range(ncore)))
    out = np.stack([np.asarray(r["x_out"], np.float32) for r in res.results])
    return out


# version 06
# speedup vs baseline: 1.0250x; 1.0250x over previous
import math
import numpy as np
import ml_dtypes
import concourse.bass as bass
import concourse.mybir as mybir
from concourse.bass_utils import run_bass_kernel_spmd
from contextlib import ExitStack

F32 = mybir.dt.float32
BF16 = mybir.dt.bfloat16
U16 = mybir.dt.uint16
U32 = mybir.dt.uint32
U8 = mybir.dt.uint8
AF = mybir.ActivationFunctionType
OP = mybir.AluOpType
AX = mybir.AxisListType
bfdt = ml_dtypes.bfloat16

P = 128
LN_EPS = 1e-5


class Cfg:
    def __init__(self, T=2048, NEXP=65536, conv_chunks=64):
        self.T = T
        self.NT = T // 128
        self.NB = max(1, T // 512)
        self.TB = T // self.NB
        self.NTB = self.TB // 128
        self.NEXP = NEXP
        self.EW = 2048
        self.D = 1024
        self.DC = 8
        self.FH = 8
        self.NK = 256
        self.DK = 512
        self.DKC = 4
        self.PH = 16
        self.NP = 8
        self.conv_chunks = conv_chunks
        self.SLOTS = 128
        self.GW = 1
        self.GR = 6


class Emit:
    ENGS = ("sync", "scalar", "vector", "tensor", "gpsimd")

    def __init__(self, nc, block, ctx, sem_names):
        self.nc = nc
        self.block = block
        self.sems = {n: ctx.enter_context(nc.semaphore(n)) for n in sem_names}
        self.c = {n: 0 for n in sem_names}
        self.last_wait = {}
        self.pending = {e: [] for e in self.ENGS}

    def flush(self):
        for eng in self.ENGS:
            ops = self.pending[eng]
            if ops:
                getattr(self.block, eng)(lambda e, ops=ops: [f(e) for f in ops])
            self.pending[eng] = []

    CHAIN = {"vector": "dv", "scalar": "ac"}

    def op(self, eng, f, waits=(), inc=None, n=1):
        waits = list(waits)
        ch = self.CHAIN.get(eng)
        if ch is not None and f is not None:
            waits.append((ch, self.c[ch]))
        wl = []
        for s, v in waits:
            if v is None or v <= 0:
                continue
            k = (eng, s)
            if self.last_wait.get(k, -1) < v:
                wl.append((self.sems[s], v))
                self.last_wait[k] = v
        if inc is not None:
            sem = self.sems[inc]
            self.c[inc] += n
            def body(e, f=f, wl=wl, sem=sem, n=n):
                for s, v in wl:
                    e.wait_ge(s, v)
                f(e).then_inc(sem, n)
            self.pending[eng].append(body)
            return self.c[inc]
        else:
            def body(e, f=f, wl=wl):
                for s, v in wl:
                    e.wait_ge(s, v)
                if f is not None:
                    f(e)
            self.pending[eng].append(body)
            return None

    def fence(self):
        snap = [(s, v) for s, v in self.c.items() if v > 0]
        for eng in self.ENGS:
            self.op(eng, None, waits=snap)


def build_nc(cfg: Cfg):
    nc = bass.Bass("TRN2", target_bir_lowering=False)
    T, NT, NB, NTB, TB = cfg.T, cfg.NT, cfg.NB, cfg.NTB, cfg.TB
    D, DC, EW, FH, DKC, NK = cfg.D, cfg.DC, cfg.EW, cfg.FH, cfg.DKC, cfg.NK
    GW, GR, SLOTS = cfg.GW, cfg.GR, cfg.SLOTS
    NQB = T // 512

    z_in = nc.dram_tensor("z_in", [T, D], F32, kind="ExternalInput").ap()
    fused32 = nc.dram_tensor("fused32", [cfg.NEXP, EW], F32, kind="ExternalInput").ap()
    wqT_d = nc.dram_tensor("wqT", [D, 2 * FH * cfg.DK], F32, kind="ExternalInput").ap()
    keysT_d = nc.dram_tensor("keysT", [cfg.PH, cfg.DK, NK], F32, kind="ExternalInput").ap()
    wq_a = nc.dram_tensor("wq_a", [D, D], F32, kind="ExternalInput").ap()
    wk_a = nc.dram_tensor("wk_a", [D, D], F32, kind="ExternalInput").ap()
    wv_a = nc.dram_tensor("wv_a", [D, D], F32, kind="ExternalInput").ap()
    w_proj = nc.dram_tensor("w_proj", [D, D], F32, kind="ExternalInput").ap()
    rowv = nc.dram_tensor("rowv", [5, D], F32, kind="ExternalInput").ap()
    ident_in = nc.dram_tensor("ident_in", [P, P], F32, kind="ExternalInput").ap()

    dbg = {}
    if getattr(cfg, "debug", False):
        for nm, shp, dt in (("dbg_x1", [T, D], F32), ("dbg_x2", [T, D], F32),
                            ("dbg_eid", [P, NT, FH, 16], U32),
                            ("dbg_wgt", [P, NT, FH, 16], F32),
                            ("dbg_xn", [P, NT, D], BF16),
                            ("dbg_zn", [P, DC, T], BF16),
                            ("dbg_q", [P, T], BF16), ("dbg_k", [P, T], BF16),
                            ("dbg_o", [P, DC, T], BF16)):
            dbg[nm] = nc.dram_tensor(nm, shp, dt, kind="ExternalOutput").ap()
    fusedbf = nc.dram_tensor("fusedbf", [cfg.NEXP, EW], BF16, kind="Internal").ap()
    negm_dram = nc.dram_tensor("negm_dram", [2, T], BF16, kind="Internal").ap()
    x_out = nc.dram_tensor("x_out", [T, D], F32, kind="ExternalOutput").ap()

    ctx = ExitStack()
    with ctx:
        def sb(name, shape, dt):
            return ctx.enter_context(nc.sbuf_tensor(name, shape, dt))

        def psum(name, shape, dt=F32):
            return ctx.enter_context(nc.psum_tensor(name, shape, dt))

        X = sb("X", [P, NT, D], F32)
        gamma_b = sb("gamma_b", [P, D], F32)
        lns_b = sb("lns_b", [P, D], F32)
        lnb_b = sb("lnb_b", [P, D], F32)
        bproj_b = sb("bproj_b", [P, D], F32)
        ramp16 = sb("ramp16", [P, 16], F32)
        ident = sb("ident", [P, P], BF16)
        ones1 = sb("ones1", [1, P], BF16)
        eid = sb("eid", [P, NT, FH, 16], U32)
        wgt = sb("wgt", [P, NT, FH, 16], F32)
        junkb = sb("junkb", [P, 1024], BF16)
        stat = sb("stat", [P, 32], F32)
        kconst = sb("kconst", [P, 2], F32)

        ar1 = sb("ar1", [P, NT * D * 2], U8)
        xn_tok = ar1[:, :].bitcast(BF16).rearrange("p (a b) -> p a b", a=NT)
        zn_fm = ar1[:, :].bitcast(BF16).rearrange("p (a b) -> p a b", a=DC)

        AR2B = 75 * 1024
        ar2 = sb("ar2", [P, AR2B], U8)
        cursor = [0]

        def carve(shape, dt):
            esz = {BF16: 2, U16: 2, F32: 4, U32: 4, U8: 1}[dt]
            nb = int(np.prod(shape[1:])) * esz
            o = cursor[0]
            assert o + nb <= AR2B, f"arena2 overflow at {o + nb}"
            cursor[0] = (o + nb + 127) // 128 * 128
            v = ar2[:, o:o + nb].bitcast(dt)
            if len(shape) == 2:
                return v
            dims = dict(zip("ab", shape[1:-1]))
            pat = " ".join("ab"[: len(shape) - 2])
            return v.rearrange(f"p ({pat} z) -> p {pat} z", **dims)

        wqs = [carve([P, DC, 512], BF16) for _ in range(2)]
        keyss = [carve([P, DKC, NK], BF16) for _ in range(2)]
        xn_fm = carve([P, DC, TB], BF16)
        qT = carve([P, DKC, TB], BF16)
        sxsy = carve([P, NTB, 2 * FH * 16], F32)
        ixiy = carve([P, NTB, 2 * FH * 16], U16)
        scr256 = carve([P, 256], F32)
        grid = carve([P, 256], F32)
        gridm = carve([P, 256], F32)
        v16 = carve([P, 16], F32)
        p16 = carve([P, 16], U16)
        absel = carve([P, 32], U16)
        ohq = carve([P, 256], F32)
        usel = carve([P, 16], F32)
        wsel = carve([P, 16], F32)
        eidf = carve([P, 16], F32)
        ew16 = carve([P, 16], F32)
        ztmp = carve([P, D], F32)
        xupd = ztmp
        gbuf = carve([P, GR * GW, EW], BF16)
        hbuf = carve([P, SLOTS], F32)
        hgel = [carve([P, GW], F32) for _ in range(2)]
        hg = carve([P, SLOTS], F32)
        diag2 = [carve([P, GW * P], BF16) for _ in range(2)]
        ffn_end = cursor[0]

        cursor[0] = 0
        O_fm = carve([P, DC, T], BF16)
        az = carve([P, D], F32)
        aupd = carve([P, D], F32)
        attn_base = cursor[0]
        qTp = carve([P, T], BF16)
        kTp = carve([P, T], BF16)
        vP = carve([P, NT, 132], BF16)
        kaug = carve([P, T], BF16)
        qaug = carve([P, T], BF16)
        expT = carve([P, 4, 512], BF16)
        wstr = [carve([P, DC, P], BF16) for _ in range(2)]
        negm2 = carve([P, 2, NT], BF16)
        m4 = carve([P, 8], F32)
        zi_sb = carve([P, 512], BF16)
        zrowf = carve([P, 512], F32)
        zrowb = carve([P, 512], BF16)
        oev = carve([P, 512], BF16)
        attn_end = cursor[0]
        cursor[0] = attn_base
        pwb = [carve([P, DC, 512], BF16) for _ in range(2)]
        proj_end = cursor[0]
        assert max(ffn_end, attn_end, proj_end) <= AR2B

        ps_tr = psum("ps_tr", [P, 1024], BF16)
        ps_q = [psum(f"ps_q{i}", [P, 512], F32) for i in range(2)]
        ps_s = [psum(f"ps_s{i}", [P, 512], F32) for i in range(2)]
        ps_acc = psum("ps_acc", [P, 1024], F32)
        ps_x1 = psum("ps_x1", [P, 512], F32)

        sems = ["ld", "st", "ag", "od", "nm", "cvt", "sw",
                "sw0", "sw1", "gt0", "gt1", "gt2", "gt3", "gt4", "gt5",
                "pe", "dv", "ac", "pl"]
        block = ctx.enter_context(nc.Block())
        E = Emit(nc, block, ctx, sems)
        c = E.c
        st8 = {}

        E.op("sync", lambda e: e.dma_start(
            out=X[:], in_=z_in.rearrange("(t p) d -> p t d", p=P)), inc="ld", n=16)
        for i, buf in enumerate((gamma_b, lns_b, lnb_b, bproj_b)):
            E.op("sync", lambda e, i=i, buf=buf: e.dma_start(
                out=buf[:], in_=rowv[i:i + 1, :].to_broadcast([P, D])), inc="ld", n=16)
        E.op("sync", lambda e: e.dma_start(
            out=ramp16[:], in_=rowv[4:5, 0:16].to_broadcast([P, 16])), inc="ld", n=16)
        ld0 = c["ld"]
        E.op("gpsimd", lambda e: e.dma_start(out=ident[:], in_=ident_in), inc="sw", n=16)
        sw0_c = c["sw"]
        E.op("gpsimd", lambda e: e.memset(ones1[:], 1.0), inc="pl", n=1)
        pl_ones = c["pl"]
        E.op("vector", lambda e: e.memset(kconst[:, 0:1], LN_EPS), inc="dv")
        kc_dv = E.op("vector", lambda e: e.memset(kconst[:, 1:2], 0.125), inc="dv")
        rows_per = cfg.NEXP // cfg.conv_chunks
        cvt_done = 16 * cfg.conv_chunks
        conv_emitted = [0]

        def emit_conv_chunks(k):
            while conv_emitted[0] < cfg.conv_chunks and k > 0:
                ch = conv_emitted[0]
                E.op("gpsimd", lambda e, ch=ch: e.dma_start(
                    out=fusedbf[ch * rows_per:(ch + 1) * rows_per, :],
                    in_=fused32[ch * rows_per:(ch + 1) * rows_per, :]),
                    inc="cvt", n=16)
                conv_emitted[0] += 1
                k -= 1

        state = {
            "ln_src": {t: [("ld", ld0)] for t in range(NT)},
            "bank": {},
        }

        def bank_w(name):
            return state["bank"].get(name, [])

        def bank_r(name, sem, val):
            state["bank"][name] = [(sem, val)]

        def emit_ln(t, zdst, extra=()):
            w = list(state["ln_src"][t])
            E.op("vector", lambda e, t=t: e.bn_stats(
                out=stat[:, 0:6], in_=X[:, t, 0:512]), waits=w, inc="dv")
            E.op("vector", lambda e, t=t: e.bn_stats(
                out=stat[:, 6:12], in_=X[:, t, 512:1024]), inc="dv")
            d1 = E.op("vector", lambda e: e.bn_aggr(
                out=stat[:, 12:14], in_=stat[:, 0:12]),
                waits=[("dv", c["dv"])], inc="dv")
            a1 = E.op("scalar", lambda e: e.activation(
                out=stat[:, 14:15], in_=stat[:, 13:14], func=AF.Sqrt,
                bias=kconst[:, 0:1]),
                waits=[("dv", d1), ("dv", kc_dv)], inc="ac")
            d2 = E.op("vector", lambda e: e.reciprocal(
                out=stat[:, 15:16], in_=stat[:, 14:15]), waits=[("ac", a1)], inc="dv")
            d3 = E.op("vector", lambda e, t=t, zdst=zdst: e.tensor_scalar(
                out=zdst[:], in0=X[:, t, :], scalar1=stat[:, 12:13],
                scalar2=stat[:, 15:16], op0=OP.subtract, op1=OP.mult),
                waits=[("dv", d2)] + list(extra), inc="dv")
            return d3

        def emit_ffn(call):
            xn_done = {}
            for t in range(NT):
                dz = emit_ln(t, ztmp, extra=[("ac", state.get("ztmp_free", 0))])
                a1 = E.op("scalar", lambda e: e.activation(
                    out=junkb[:, 0:D], in_=ztmp[:], func=AF.Square,
                    accum_out=stat[:, 16:17]), waits=[("dv", dz)], inc="ac")
                state["ztmp_free"] = a1
                a2 = E.op("scalar", lambda e: e.activation(
                    out=stat[:, 17:18], in_=stat[:, 16:17], func=AF.Sqrt),
                    waits=[("ac", a1)], inc="ac")
                d1 = E.op("vector", lambda e: e.tensor_scalar(
                    out=stat[:, 18:19], in0=stat[:, 17:18], scalar1=1e-12,
                    scalar2=None, op0=OP.max, op1=OP.bypass),
                    waits=[("ac", a2)], inc="dv")
                d2 = E.op("vector", lambda e: e.reciprocal(
                    out=stat[:, 19:20], in_=stat[:, 18:19]), waits=[("dv", d1)], inc="dv")
                d3 = E.op("vector", lambda e: e.tensor_scalar(
                    out=stat[:, 20:21], in0=stat[:, 19:20],
                    scalar1=float(math.sqrt(D)), scalar2=None,
                    op0=OP.mult, op1=OP.bypass), waits=[("dv", d2)], inc="dv")
                d4 = E.op("vector", lambda e, t=t: e.scalar_tensor_tensor(
                    out=xn_tok[:, t, :], in0=ztmp[:], scalar=stat[:, 20:21],
                    in1=gamma_b[:], op0=OP.mult, op1=OP.mult),
                    waits=[("dv", d3)], inc="dv")
                xn_done[t] = d4

            def emit_bc(b):
                for tt in range(NTB):
                    t = b * NTB + tt
                    for cc in range(DC):
                        E.op("tensor", lambda e, t=t, cc=cc: e.transpose(
                            ps_tr[:, cc * 128:(cc + 1) * 128],
                            xn_tok[:, t, cc * 128:(cc + 1) * 128], ident[:]),
                            waits=[("dv", xn_done[t]), ("sw", sw0_c)] + bank_w("tr"),
                            inc="pe")
                    ev = E.op("vector", lambda e, tt=tt: e.tensor_copy(
                        out=xn_fm[:, :, tt * 128:(tt + 1) * 128],
                        in_=ps_tr[:].rearrange("p (a z) -> p a z", a=DC)),
                        waits=[("pe", c["pe"])], inc="dv")
                    bank_r("tr", "dv", ev)
                xn_fm_done = c["dv"]

                eid_done = {}
                for ph in range(cfg.PH):
                    par = ph % 2
                    gsem = "sw0" if par == 0 else "sw1"
                    E.op("gpsimd", lambda e, ph=ph, par=par: e.dma_start(
                        out=wqs[par][:, :, :],
                        in_=wqT_d[:, ph * 512:(ph + 1) * 512]
                            .rearrange("(a p) m -> p a m", p=P)),
                        waits=[("pe", state.get(("wqfree", par), 0))],
                        inc=gsem, n=16)
                    E.op("gpsimd", lambda e, ph=ph, par=par: e.dma_start(
                        out=keyss[par][:, :, :],
                        in_=keysT_d[ph].rearrange("(a p) m -> p a m", p=P)),
                        inc=gsem, n=16)
                    gval = c[gsem]
                    emit_conv_chunks(2)

                    qdone = [0] * DKC
                    for dkc in range(DKC):
                        qp = dkc % 2
                        for kc in range(DC):
                            E.op("tensor", lambda e, dkc=dkc, kc=kc, par=par, qp=qp:
                                 e.matmul(ps_q[qp][:, 0:TB],
                                          wqs[par][:, kc, dkc * 128:(dkc + 1) * 128],
                                          xn_fm[:, kc, :],
                                          start=(kc == 0), stop=(kc == DC - 1)),
                                 waits=[(gsem, gval), ("dv", xn_fm_done)]
                                 + bank_w(f"q{qp}"), inc="pe")
                        ev = E.op("scalar", lambda e, dkc=dkc, qp=qp: e.activation(
                            out=qT[:, dkc, :], in_=ps_q[qp][:, 0:TB], func=AF.Copy),
                            waits=[("pe", c["pe"])], inc="ac")
                        bank_r(f"q{qp}", "ac", ev)
                        qdone[dkc] = ev

                    p_, h_ = ph // FH, ph % FH
                    for tt in range(NTB):
                        sp = tt % 2
                        for dkc in range(DKC):
                            pe_l = E.op("tensor",
                                 lambda e, tt=tt, dkc=dkc, par=par, sp=sp:
                                 e.matmul(ps_s[sp][:, 0:NK],
                                          qT[:, dkc, tt * 128:(tt + 1) * 128],
                                          keyss[par][:, dkc, :],
                                          start=(dkc == 0), stop=(dkc == DKC - 1)),
                                 waits=[("ac", qdone[dkc]), (gsem, gval)]
                                 + bank_w(f"s{sp}"), inc="pe")
                        o1 = (p_ * FH + h_) * 16
                        d1 = E.op("vector", lambda e, tt=tt, sp=sp, o1=o1:
                             e.max(out=sxsy[:, tt, o1:o1 + 8], in_=ps_s[sp][:, 0:NK]),
                             waits=[("pe", c["pe"])], inc="dv")
                        d2 = E.op("vector", lambda e, tt=tt, sp=sp, o1=o1:
                             e.max_index(out=ixiy[:, tt, o1:o1 + 8],
                                         in_max=sxsy[:, tt, o1:o1 + 8],
                                         in_values=ps_s[sp][:, 0:NK]),
                             waits=[("dv", d1)], inc="dv")
                        d3 = E.op("vector", lambda e, tt=tt, sp=sp, o1=o1:
                             e.match_replace(out=scr256[:],
                                             in_to_replace=sxsy[:, tt, o1:o1 + 8],
                                             in_values=ps_s[sp][:, 0:NK],
                                             imm_value=-1e30),
                             waits=[("dv", d2)], inc="dv")
                        bank_r(f"s{sp}", "dv", d3)
                        d4 = E.op("vector", lambda e, tt=tt, o1=o1:
                             e.max(out=sxsy[:, tt, o1 + 8:o1 + 16], in_=scr256[:]),
                             waits=[("dv", d3)], inc="dv")
                        E.op("vector", lambda e, tt=tt, o1=o1:
                             e.max_index(out=ixiy[:, tt, o1 + 8:o1 + 16],
                                         in_max=sxsy[:, tt, o1 + 8:o1 + 16],
                                         in_values=scr256[:]),
                             waits=[("dv", d4)], inc="dv")
                    state[("wqfree", par)] = c["pe"]

                stage1_done = c["dv"]
                for tt in range(NTB):
                    t = b * NTB + tt
                    for h_ in range(FH):
                        ox = (0 * FH + h_) * 16
                        oy = (1 * FH + h_) * 16
                        g1 = E.op("vector", lambda e, tt=tt, ox=ox, oy=oy:
                             e.tensor_tensor(
                                 out=grid[:],
                                 in0=sxsy[:, tt, ox:ox + 16].unsqueeze(2)
                                     .to_broadcast([P, 16, 16]),
                                 in1=sxsy[:, tt, oy:oy + 16].unsqueeze(1)
                                     .to_broadcast([P, 16, 16]),
                                 op=OP.add),
                             waits=[("dv", stage1_done)], inc="dv")
                        g2 = E.op("vector", lambda e: e.max(
                            out=v16[:, 0:8], in_=grid[:]), waits=[("dv", g1)], inc="dv")
                        g3 = E.op("vector", lambda e: e.max_index(
                            out=p16[:, 0:8], in_max=v16[:, 0:8], in_values=grid[:]),
                            waits=[("dv", g2)], inc="dv")
                        g4 = E.op("vector", lambda e: e.match_replace(
                            out=gridm[:], in_to_replace=v16[:, 0:8],
                            in_values=grid[:], imm_value=-1e30),
                            waits=[("dv", g3)], inc="dv")
                        g5 = E.op("vector", lambda e: e.max(
                            out=v16[:, 8:16], in_=gridm[:]), waits=[("dv", g4)], inc="dv")
                        g6 = E.op("vector", lambda e: e.max_index(
                            out=p16[:, 8:16], in_max=v16[:, 8:16], in_values=gridm[:]),
                            waits=[("dv", g5)], inc="dv")
                        g7 = E.op("vector", lambda e: e.tensor_scalar(
                            out=absel[:, 0:16], in0=p16[:], scalar1=4, scalar2=None,
                            op0=OP.logical_shift_right, op1=OP.bypass),
                            waits=[("dv", g6)], inc="dv")
                        g8 = E.op("vector", lambda e: e.tensor_scalar(
                            out=absel[:, 16:32], in0=p16[:], scalar1=15, scalar2=None,
                            op0=OP.bitwise_and, op1=OP.bypass), inc="dv")
                        g9 = E.op("vector", lambda e: e.tensor_tensor(
                            out=ohq[:],
                            in0=absel[:, 0:16].unsqueeze(2).to_broadcast([P, 16, 16]),
                            in1=ramp16[:].unsqueeze(1).to_broadcast([P, 16, 16]),
                            op=OP.is_equal), waits=[("dv", g7)], inc="dv")
                        g10 = E.op("vector", lambda e, tt=tt, ox=ox: e.tensor_tensor(
                            out=ohq[:], in0=ohq[:],
                            in1=ixiy[:, tt, ox:ox + 16].unsqueeze(1)
                                .to_broadcast([P, 16, 16]),
                            op=OP.mult), waits=[("dv", g9)], inc="dv")
                        g11 = E.op("vector", lambda e: e.tensor_reduce(
                            out=usel[:], in_=ohq[:].rearrange("p (a z) -> p a z", a=16),
                            axis=AX.X, op=OP.add), waits=[("dv", g10)], inc="dv")
                        g12 = E.op("vector", lambda e: e.tensor_tensor(
                            out=ohq[:],
                            in0=absel[:, 16:32].unsqueeze(2).to_broadcast([P, 16, 16]),
                            in1=ramp16[:].unsqueeze(1).to_broadcast([P, 16, 16]),
                            op=OP.is_equal),
                            waits=[("dv", g11), ("dv", g8)], inc="dv")
                        g13 = E.op("vector", lambda e, tt=tt, oy=oy: e.tensor_tensor(
                            out=ohq[:], in0=ohq[:],
                            in1=ixiy[:, tt, oy:oy + 16].unsqueeze(1)
                                .to_broadcast([P, 16, 16]),
                            op=OP.mult), waits=[("dv", g12)], inc="dv")
                        g14 = E.op("vector", lambda e: e.tensor_reduce(
                            out=wsel[:], in_=ohq[:].rearrange("p (a z) -> p a z", a=16),
                            axis=AX.X, op=OP.add), waits=[("dv", g13)], inc="dv")
                        g15 = E.op("vector", lambda e: e.scalar_tensor_tensor(
                            out=eidf[:], in0=usel[:], scalar=256.0, in1=wsel[:],
                            op0=OP.mult, op1=OP.add), waits=[("dv", g14)], inc="dv")
                        g16 = E.op("vector", lambda e, t=t, h_=h_: e.tensor_copy(
                            out=eid[:, t, h_, :], in_=eidf[:]),
                            waits=[("dv", g15)] + [
                                (f"gt{r}", st8.get(("eidfree", t, h_, r), 0))
                                for r in range(GR)],
                            inc="dv")
                        g17 = E.op("vector", lambda e: e.tensor_scalar(
                            out=stat[:, 21:22], in0=v16[:, 0:1], scalar1=-1.0,
                            scalar2=None, op0=OP.mult, op1=OP.bypass),
                            waits=[("dv", g6)], inc="dv")
                        a1 = E.op("scalar", lambda e: e.activation(
                            out=ew16[:], in_=v16[:], func=AF.Exp, bias=stat[:, 21:22],
                            accum_out=stat[:, 22:23]), waits=[("dv", g17)], inc="ac")
                        dz = E.op("vector", lambda e: e.reciprocal(
                            out=stat[:, 23:24], in_=stat[:, 22:23]),
                            waits=[("ac", a1)], inc="dv")
                        g18 = E.op("vector", lambda e, t=t, h_=h_: e.tensor_scalar(
                            out=wgt[:, t, h_, :], in0=ew16[:],
                            scalar1=stat[:, 23:24], scalar2=None,
                            op0=OP.mult, op1=OP.bypass), waits=[("dv", dz)], inc="dv")
                        eid_done[(t, h_)] = max(g16, g18)
                return eid_done

            def emit_d(b, eid_done):
                for tt in range(NTB):
                    t = b * NTB + tt
                    edone = max(eid_done[(t, h_)] for h_ in range(FH))
                    nw = SLOTS // GW
                    wflat = wgt[:, t, :, :].rearrange("p a z -> p (a z)")

                    def emit_tail(wv, t=t, wflat=wflat, edone=edone):
                        pp = wv % 2
                        s0 = wv * GW
                        ring = wv % GR
                        a1 = state[("gelu", wv % 4)]
                        d5 = E.op("vector", lambda e, s0=s0, wflat=wflat, pp=pp:
                             e.tensor_tensor(
                                 out=hg[:, s0:s0 + GW], in0=hgel[pp][:],
                                 in1=wflat[:, s0:s0 + GW], op=OP.mult),
                             waits=[("ac", a1), ("dv", edone)], inc="dv")
                        if GW == 1:
                            d6 = E.op("scalar", lambda e, s0=s0, pp=pp:
                                 e.activation(
                                     out=diag2[pp][:], in_=ident[:], func=AF.Copy,
                                     scale=hg[:, s0:s0 + 1]),
                                 waits=[("dv", d5),
                                        ("pe", state.get(("diagfree", pp), 0))],
                                 inc="ac")
                            d6w = ("ac", d6)
                        else:
                            d6 = E.op("vector", lambda e, s0=s0, pp=pp:
                                 e.scalar_tensor_tensor(
                                     out=diag2[pp][:],
                                     in0=ident[:].unsqueeze(1).to_broadcast([P, GW, P]),
                                     scalar=1.0,
                                     in1=hg[:, s0:s0 + GW].unsqueeze(2)
                                         .to_broadcast([P, GW, P]),
                                     op0=OP.mult, op1=OP.mult),
                                 waits=[("dv", d5),
                                        ("pe", state.get(("diagfree", pp), 0))],
                                 inc="dv")
                            d6w = ("dv", d6)
                        for j in range(GW):
                            s = wv * GW + j
                            for half in range(2):
                                E.op("tensor",
                                     lambda e, ring=ring, j=j, s=s, half=half, pp=pp:
                                     e.matmul(ps_acc[:, half * 512:(half + 1) * 512],
                                              diag2[pp][:, j * P:(j + 1) * P],
                                              gbuf[:, ring * GW + j,
                                                   D + half * 512:D + (half + 1) * 512],
                                              start=(s == 0), stop=(s == SLOTS - 1),
                                              skip_group_check=True),
                                     waits=[d6w] +
                                     (bank_w("acc") if s == 0 and half == 0 else []),
                                     inc="pe")
                        state[("gslot", ring)] = (state[("dots", ring)], c["pe"])
                        state[("diagfree", pp)] = c["pe"]

                    for wv in range(nw):
                        ring = wv % GR
                        gsem = f"gt{ring}"
                        war = state.get(("gslot", ring))
                        waits = [("cvt", cvt_done), ("dv", edone)]
                        if war:
                            waits += [("dv", war[0]), ("pe", war[1])]
                        for j in range(GW):
                            s = wv * GW + j
                            h_, k_ = s // 16, s % 16
                            E.op("gpsimd", lambda e, t=t, h_=h_, k_=k_, ring=ring, j=j:
                                 e.indirect_dma_start(
                                     out=gbuf[:, ring * GW + j, :], out_offset=None,
                                     in_=fusedbf,
                                     in_offset=bass.IndirectOffsetOnAxis(
                                         ap=eid[:, t, h_, k_:k_ + 1], axis=0)),
                                 waits=waits, inc=gsem, n=16)
                            waits = []
                            st8[("eidfree", t, h_, ring)] = c[gsem]
                        gval = c[gsem]
                        dots = 0
                        for j in range(GW):
                            s = wv * GW + j
                            dots = E.op("vector", lambda e, t=t, ring=ring, j=j, s=s:
                                 e.scalar_tensor_tensor(
                                     out=junkb[:, 0:D],
                                     in0=gbuf[:, ring * GW + j, 0:D],
                                     scalar=1.0, in1=xn_tok[:, t, :],
                                     op0=OP.mult, op1=OP.mult,
                                     accum_out=hbuf[:, s:s + 1]),
                                 waits=[(gsem, gval), ("dv", xn_done[t]),
                                        ("ac", state.get(("hfree", wv), 0))],
                                 inc="dv")
                        state[("dots", ring)] = dots
                        s0 = wv * GW
                        a1 = E.op("scalar", lambda e, s0=s0, pp=wv % 2: e.activation(
                            out=hgel[pp][:], in_=hbuf[:, s0:s0 + GW], func=AF.Gelu),
                            waits=[("dv", dots),
                                   ("dv", state.get(("hgelfree", wv % 2), 0))],
                            inc="ac")
                        state[("hfree", wv)] = a1
                        state[("gelu", wv % 4)] = a1
                        if wv > 0:
                            emit_tail(wv - 1)
                            state[("hgelfree", (wv - 1) % 2)] = c["dv"]
                    emit_tail(nw - 1)
                    up_done = c["pe"]
                    d7 = E.op("vector", lambda e, t=t: e.scalar_tensor_tensor(
                        out=xupd[:], in0=ps_acc[:], scalar=0.5, in1=X[:, t, :],
                        op0=OP.mult, op1=OP.add), waits=[("pe", up_done)], inc="dv")
                    bank_r("acc", "dv", d7)
                    d8 = E.op("vector", lambda e, t=t: e.tensor_copy(
                        out=X[:, t, :], in_=xupd[:]), waits=[("dv", d7)], inc="dv")
                    state["ln_src"][t] = [("dv", d8)]

            ed = [None] * NB
            if call == 0:
                for b in range(NB):
                    ed[b] = emit_bc(b)
                for b in range(NB):
                    emit_d(b, ed[b])
            else:
                ed[0] = emit_bc(0)
                for b in range(NB):
                    if b + 1 < NB:
                        ed[b + 1] = emit_bc(b + 1)
                    emit_d(b, ed[b])

        def emit_attn():
            zn_done = 0
            for t in range(NT):
                dz = emit_ln(t, az)
                d1 = E.op("vector", lambda e, t=t: e.tensor_tensor(
                    out=aupd[:], in0=X[:, t, :], in1=az[:], op=OP.add),
                    waits=[("dv", dz)], inc="dv")
                d2 = E.op("vector", lambda e, t=t: e.tensor_copy(
                    out=X[:, t, :], in_=aupd[:]), waits=[("dv", d1)], inc="dv")
                state["ln_src"][t] = [("dv", d2)]
                E.op("vector", lambda e: e.bn_stats(out=stat[:, 0:6], in_=az[:, 0:512]),
                     waits=[("dv", dz)], inc="dv")
                E.op("vector", lambda e: e.bn_stats(out=stat[:, 6:12], in_=az[:, 512:1024]),
                     inc="dv")
                d3 = E.op("vector", lambda e: e.bn_aggr(
                    out=stat[:, 12:14], in_=stat[:, 0:12]),
                    waits=[("dv", c["dv"])], inc="dv")
                a1 = E.op("scalar", lambda e: e.activation(
                    out=stat[:, 14:15], in_=stat[:, 13:14], func=AF.Sqrt,
                    bias=kconst[:, 0:1]),
                    waits=[("dv", d3), ("dv", kc_dv)], inc="ac")
                d4 = E.op("vector", lambda e: e.reciprocal(
                    out=stat[:, 15:16], in_=stat[:, 14:15]), waits=[("ac", a1)], inc="dv")
                d5 = E.op("vector", lambda e: e.tensor_scalar(
                    out=aupd[:], in0=az[:], scalar1=stat[:, 12:13],
                    scalar2=stat[:, 15:16], op0=OP.subtract, op1=OP.mult),
                    waits=[("dv", d4), ("dv", d2)], inc="dv")
                d6 = E.op("vector", lambda e: e.tensor_tensor(
                    out=aupd[:], in0=aupd[:], in1=lns_b[:], op=OP.mult),
                    waits=[("dv", d5)], inc="dv")
                d7 = E.op("vector", lambda e: e.tensor_tensor(
                    out=junkb[:, 0:D], in0=aupd[:], in1=lnb_b[:], op=OP.add),
                    waits=[("dv", d6), ("pe", state.get("znj_free", 0))], inc="dv")
                for cc in range(DC):
                    E.op("tensor", lambda e, cc=cc: e.transpose(
                        ps_tr[:, cc * 128:(cc + 1) * 128],
                        junkb[:, cc * 128:(cc + 1) * 128], ident[:]),
                        waits=[("dv", d7), ("sw", sw0_c)] + bank_w("tr"), inc="pe")
                state["znj_free"] = c["pe"]
                ev = E.op("vector", lambda e, t=t: e.tensor_copy(
                    out=zn_fm[:, :, t * 128:(t + 1) * 128],
                    in_=ps_tr[:].rearrange("p (a z) -> p a z", a=DC)),
                    waits=[("pe", c["pe"])], inc="dv")
                bank_r("tr", "dv", ev)
                zn_done = ev

            for hp in range(cfg.NP):
                par = hp % 2
                gsem = "sw0" if par == 0 else "sw1"
                qk_ev = {}
                for which, (wt, buf, scl) in enumerate(
                        ((wq_a, qTp, "q"), (wk_a, kTp, None))):
                    E.op("gpsimd", lambda e, wt=wt, hp=hp, par=par: e.dma_start(
                        out=wstr[par][:, :, :],
                        in_=wt[:, hp * 128:(hp + 1) * 128]
                            .rearrange("(a p) m -> p a m", p=P)),
                        waits=[("pe", c["pe"])], inc=gsem, n=16)
                    gval = c[gsem]
                    for tb in range(NQB):
                        qp = tb % 2
                        for kc in range(DC):
                            E.op("tensor", lambda e, tb=tb, kc=kc, par=par, qp=qp:
                                 e.matmul(ps_q[qp][:],
                                          wstr[par][:, kc, :],
                                          zn_fm[:, kc, tb * 512:(tb + 1) * 512],
                                          start=(kc == 0), stop=(kc == DC - 1)),
                                 waits=[(gsem, gval), ("dv", zn_done)]
                                 + bank_w(f"q{qp}"), inc="pe")
                        if scl == "q":
                            ev = E.op("scalar", lambda e, tb=tb, buf=buf, qp=qp:
                                      e.activation(out=buf[:, tb * 512:(tb + 1) * 512],
                                                   in_=ps_q[qp][:], func=AF.Copy,
                                                   scale=kconst[:, 1:2]),
                                      waits=[("pe", c["pe"]), ("dv", kc_dv)], inc="ac")
                        else:
                            ev = E.op("scalar", lambda e, tb=tb, buf=buf, qp=qp:
                                      e.activation(out=buf[:, tb * 512:(tb + 1) * 512],
                                                   in_=ps_q[qp][:], func=AF.Copy),
                                      waits=[("pe", c["pe"])], inc="ac")
                        bank_r(f"q{qp}", "ac", ev)
                    state[("wsfree", par)] = c["pe"]
                    qk_ev[which] = c["ac"]
                agw = [("ac", qk_ev[0]), ("ac", qk_ev[1]),
                       ("pe", state.get("augfree", 0))]
                E.op("sync", lambda e: e.dma_start(
                    out=qaug[0:64, :], in_=qTp[0:64, :]), waits=agw, inc="ag", n=16)
                E.op("sync", lambda e: e.dma_start(
                    out=kaug[0:64, :], in_=kTp[0:64, :]), inc="ag", n=16)
                aug_ld = c["ag"]
                ko = E.op("vector", lambda e: e.memset(kaug[64:65, :], 1.0),
                          waits=[("ag", aug_ld), ("pe", state.get("augfree", 0))],
                          inc="dv")

                sbanks = {0: ("q0", ps_q[0]), 1: ("q1", ps_q[1])}
                sbanks2 = {0: ("s0", ps_s[0]), 1: ("s1", ps_s[1])}
                for t in range(NT):
                    rm_last = {}
                    for tkb in range(NQB):
                        for side in range(2):
                            bname, pss = (sbanks if tkb % 2 == 0 else sbanks2)[side]
                            pv = E.op("tensor",
                                 lambda e, t=t, tkb=tkb, side=side, pss=pss:
                                 e.matmul(pss[:],
                                          qTp[side * 64:(side + 1) * 64,
                                              t * 128:(t + 1) * 128],
                                          kTp[side * 64:(side + 1) * 64,
                                              tkb * 512:(tkb + 1) * 512],
                                          start=True, stop=True,
                                          tile_position=(side * 64, 0)),
                                 waits=[("ac", qk_ev[1]), ("ag", aug_ld)]
                                 + bank_w(bname), inc="pe")
                            rm = E.op("vector",
                                 lambda e, t=t, tkb=tkb, side=side, pss=pss:
                                 e.tensor_scalar(
                                     out=junkb[:, 512:1024], in0=pss[:],
                                     scalar1=1.0, scalar2=-1e30,
                                     op0=OP.mult, op1=OP.max,
                                     accum_out=m4[:, side * 4 + tkb:
                                                  side * 4 + tkb + 1]),
                                 waits=[("pe", pv)], inc="dv")
                            bank_r(bname, "dv", rm)
                            rm_last[side] = rm
                    for side in range(2):
                        dmx = E.op("vector", lambda e, side=side: e.tensor_reduce(
                            out=stat[:, 24:25],
                            in_=m4[:, side * 4:side * 4 + NQB],
                            axis=AX.X, op=OP.max),
                            waits=[("dv", rm_last[side])], inc="dv")
                        E.op("vector", lambda e, t=t, side=side: e.tensor_scalar(
                            out=negm2[:, side, t:t + 1], in0=stat[:, 24:25],
                            scalar1=-1.0, scalar2=None, op0=OP.mult, op1=OP.bypass),
                            waits=[("dv", dmx), ("nm", state.get("negm_free", 0))],
                            inc="dv")
                negm_dv = c["dv"]
                for side in range(2):
                    E.op("sync", lambda e, side=side: e.dma_start(
                        out=negm_dram[side].rearrange("(t p) -> p t", p=P),
                        in_=negm2[:, side, :],
                        allow_slow_non_contiguous=True),
                        waits=[("dv", negm_dv)], inc="nm", n=16)
                nm_st = c["nm"]
                state["negm_free"] = nm_st

                E.op("gpsimd", lambda e, hp=hp, par=par: e.dma_start(
                    out=wstr[par][:, :, :],
                    in_=wv_a[:, hp * 128:(hp + 1) * 128]
                        .rearrange("(a p) m -> p a m", p=P)),
                    waits=[("pe", c["pe"])], inc=gsem, n=16)
                gval = c[gsem]
                dvo = E.op("vector", lambda e: e.memset(vP[:, :, 64:65], 1.0),
                           waits=[("pe", state.get("pvfree", 0))], inc="dv")
                dvo = E.op("vector", lambda e: e.memset(vP[:, :, 130:131], 1.0),
                           inc="dv")
                for t in range(NT):
                    qp = t % 2
                    for kc in range(DC):
                        E.op("tensor", lambda e, t=t, kc=kc, par=par, qp=qp:
                             e.matmul(ps_q[qp][:, 0:128],
                                      zn_fm[:, kc, t * 128:(t + 1) * 128],
                                      wstr[par][:, kc, :],
                                      start=(kc == 0), stop=(kc == DC - 1)),
                             waits=[(gsem, gval), ("dv", zn_done), ("dv", dvo)]
                             + bank_w(f"q{qp}"), inc="pe")
                    e1 = E.op("vector", lambda e, t=t, qp=qp: e.tensor_copy(
                        out=vP[:, t, 0:64], in_=ps_q[qp][:, 0:64]),
                        waits=[("pe", c["pe"])], inc="dv")
                    e2 = E.op("vector", lambda e, t=t, qp=qp: e.tensor_copy(
                        out=vP[:, t, 66:130], in_=ps_q[qp][:, 64:128]), inc="dv")
                    bank_r(f"q{qp}", "dv", e2)
                v_done = c["dv"]
                state[("wsfree", par)] = c["pe"]

                for side in range(2):
                    if side == 1:
                        agw2 = [("pe", c["pe"])]
                        E.op("sync", lambda e: e.dma_start(
                            out=qaug[0:64, :], in_=qTp[64:128, :]),
                            waits=agw2, inc="ag", n=16)
                        E.op("sync", lambda e: e.dma_start(
                            out=kaug[0:64, :], in_=kTp[64:128, :]), inc="ag", n=16)
                        aug_ld2 = c["ag"]
                        ko = E.op("vector", lambda e: e.memset(kaug[64:65, :], 1.0),
                                  waits=[("ag", aug_ld2)], inc="dv")
                    nrow = E.op("sync", lambda e, side=side: e.dma_start(
                        out=qaug[64:65, :], in_=negm_dram[side:side + 1, :]),
                        waits=[("nm", nm_st),
                               ("pe", c["pe"] if side == 1
                                else state.get("augfree", 0)),
                               ("ag", c["ag"])], inc="nm", n=16)
                    nm_row = c["nm"]
                    vofs = 0 if side == 0 else 66
                    h = 2 * hp + side
                    chunk, half = hp, side
                    for qb in range(NQB):
                        aes = {}
                        pvpar = (side * NQB + qb) % 2
                        pvps = ps_x1 if pvpar == 0 else ps_s[0]
                        pvbn = "x1" if pvpar == 0 else "s0"

                        def emit_pv(tkt, vofs=vofs, pvps=pvps, pvbn=pvbn):
                            er = tkt % 4
                            pv2 = E.op("tensor", lambda e, tkt=tkt, er=er,
                                       vofs=vofs, pvps=pvps:
                                 e.matmul(pvps[0:65, :],
                                          vP[:, tkt, vofs:vofs + 65],
                                          expT[:, er, :],
                                          start=(tkt == 0), stop=(tkt == NT - 1),
                                          skip_group_check=True),
                                 waits=[("ac", aes[tkt]), ("dv", v_done)]
                                 + (bank_w(pvbn) if tkt == 0 else []), inc="pe")
                            state[("pvc", er)] = pv2
                            return pv2

                        pvdone = 0
                        for tkt in range(NT):
                            er = tkt % 4
                            stp = tkt % 2
                            pv = E.op("tensor", lambda e, tkt=tkt, qb=qb, stp=stp:
                                 e.matmul(ps_acc[:, stp * 512:(stp + 1) * 512],
                                          kaug[0:65, tkt * 128:(tkt + 1) * 128],
                                          qaug[0:65, qb * 512:(qb + 1) * 512],
                                          start=True, stop=True),
                                 waits=[("nm", nm_row), ("dv", ko)]
                                 + bank_w(f"a{stp}"), inc="pe")
                            ae = E.op("scalar", lambda e, er=er, stp=stp:
                                 e.activation(out=expT[:, er, :],
                                              in_=ps_acc[:, stp * 512:(stp + 1) * 512],
                                              func=AF.Exp),
                                 waits=[("pe", pv), ("pe", state.get(("pvc", er), 0))],
                                 inc="ac")
                            bank_r(f"a{stp}", "ac", ae)
                            aes[tkt] = ae
                            if tkt >= 1:
                                pvdone = emit_pv(tkt - 1)
                        pvdone = emit_pv(NT - 1)
                        r1 = E.op("vector", lambda e, pvps=pvps: e.reciprocal(
                            out=zrowf[0:1, :], in_=pvps[64:65, :]),
                            waits=[("pe", pvdone)], inc="dv")
                        r2 = E.op("vector", lambda e: e.tensor_copy(
                            out=zrowb[0:1, :], in_=zrowf[0:1, :]),
                            waits=[("dv", r1)], inc="dv")
                        bz = E.op("tensor", lambda e: e.matmul(
                            ps_q[0][0:64, :], ones1[:, 0:64], zrowb[0:1, :],
                            start=True, stop=True),
                            waits=[("dv", r2), ("pl", pl_ones)]
                            + bank_w("q0"), inc="pe")
                        ze = E.op("vector", lambda e: e.tensor_copy(
                            out=zi_sb[0:64, :], in_=ps_q[0][0:64, :]),
                            waits=[("pe", bz)], inc="dv")
                        bank_r("q0", "dv", ze)
                        oe = E.op("vector", lambda e, pvps=pvps: e.tensor_tensor(
                            out=oev[0:64, :], in0=pvps[0:64, :],
                            in1=zi_sb[0:64, :], op=OP.mult),
                            waits=[("dv", ze), ("od", state.get("oev_free", 0))],
                            inc="dv")
                        bank_r(pvbn, "dv", oe)
                        E.op("sync", lambda e, qb=qb, chunk=chunk, half=half:
                             e.dma_start(
                                 out=O_fm[half * 64:(half + 1) * 64, chunk,
                                          qb * 512:(qb + 1) * 512],
                                 in_=oev[0:64, :]),
                             waits=[("dv", oe)], inc="od", n=16)
                        state["oev_free"] = c["od"]
                state["augfree"] = c["pe"]
                state["pvfree"] = c["pe"]
            o_all = c["od"]

            for half in range(2):
                E.op("gpsimd", lambda e, half=half: e.dma_start(
                    out=pwb[half][:, :, :],
                    in_=w_proj[:, half * 512:(half + 1) * 512]
                        .rearrange("(a p) m -> p a m", p=P)),
                    waits=[("pe", c["pe"]), ("dv", c["dv"]), ("od", c["od"]),
                           ("ac", c["ac"]), ("nm", c["nm"]), ("ag", c["ag"])],
                    inc="sw", n=16)
            pw_ld = c["sw"]
            for t in range(NT):
                xw = []
                for half in range(2):
                    hp_ = half
                    bname = "q0" if half == 0 else "q1"
                    pss = ps_q[half]
                    for kc in range(DC):
                        E.op("tensor", lambda e, t=t, kc=kc, half=half, pss=pss:
                             e.matmul(pss[:],
                                      O_fm[:, kc, t * 128:(t + 1) * 128],
                                      pwb[half][:, kc, :],
                                      start=(kc == 0), stop=(kc == DC - 1)),
                             waits=[("sw", pw_ld), ("od", o_all)]
                             + bank_w(bname), inc="pe")
                    e1 = E.op("vector", lambda e, t=t, half=half, pss=pss:
                         e.scalar_tensor_tensor(
                             out=aupd[:, half * 512:(half + 1) * 512],
                             in0=pss[:], scalar=1.0,
                             in1=bproj_b[:, half * 512:(half + 1) * 512],
                             op0=OP.mult, op1=OP.add),
                         waits=[("pe", c["pe"])], inc="dv")
                    bank_r(bname, "dv", e1)
                    e2 = E.op("vector", lambda e, t=t, half=half: e.tensor_tensor(
                        out=az[:, half * 512:(half + 1) * 512],
                        in0=aupd[:, half * 512:(half + 1) * 512],
                        in1=X[:, t, half * 512:(half + 1) * 512], op=OP.add),
                        waits=[("dv", e1)] + state["ln_src"][t], inc="dv")
                    xw.append(e2)
                e3 = E.op("vector", lambda e, t=t: e.tensor_copy(
                    out=X[:, t, :], in_=az[:]),
                    waits=[("dv", max(xw))], inc="dv")
                state["ln_src"][t] = [("dv", e3)]

        phases = getattr(cfg, "phases", "faf")
        def dump(name, src):
            if name in dbg:
                tgt = dbg[name]
                if name in ("dbg_x1", "dbg_x2"):
                    tgt = tgt.rearrange("(t p) d -> p t d", p=P)
                E.op("sync", lambda e, tgt=tgt, src=src: e.dma_start(
                    out=tgt, in_=src),
                    waits=[(k, v) for k, v in c.items() if v > 0], inc="st", n=16)

        emit_ffn(0)
        E.fence()
        dump("dbg_x1", X[:].rearrange("p t d -> t p d") if False else X[:])
        dump("dbg_eid", eid[:])
        dump("dbg_wgt", wgt[:])
        dump("dbg_xn", xn_tok[:, :, :])
        E.fence()
        if len(phases) > 1:
            emit_attn()
            E.fence()
            dump("dbg_x2", X[:])
            dump("dbg_zn", zn_fm[:, :, :])
            dump("dbg_o", O_fm[:, :, :])
            dump("dbg_q", qTp[:])
            dump("dbg_k", kTp[:])
            E.fence()
        if len(phases) > 2:
            emit_ffn(1)

        stw = [w for t in range(NT) for w in state["ln_src"][t]]
        E.op("sync", lambda e: e.dma_start(
            out=x_out.rearrange("(t p) d -> p t d", p=P), in_=X[:]),
            waits=stw, inc="st", n=16)
        E.op("sync", None, waits=[("st", c["st"])])
        E.flush()

    return nc



def _host_prep(inputs, cfg):
    D = cfg.D
    w_qkv = np.asarray(inputs["L0_attn_w_qkv"], dtype=np.float32)
    H, HD = 16, 64
    cols = np.arange(D)
    h, hd = cols // HD, cols % HD
    wq_a = np.ascontiguousarray(w_qkv[:, h * (HD * 3) + hd * 3 + 0])
    wk_a = np.ascontiguousarray(w_qkv[:, h * (HD * 3) + hd * 3 + 1])
    wv_a = np.ascontiguousarray(w_qkv[:, h * (HD * 3) + hd * 3 + 2])
    wqT = np.ascontiguousarray(np.asarray(inputs["L0_ffn_wq"], np.float32).T)
    keys = np.asarray(inputs["L0_ffn_keys"], np.float32)
    keysT = np.ascontiguousarray(
        keys.transpose(2, 0, 3, 1).reshape(cfg.PH, cfg.DK, cfg.NK))
    fused32 = np.ascontiguousarray(np.concatenate(
        [np.asarray(inputs["L0_ffn_table_down"], np.float32),
         np.asarray(inputs["L0_ffn_table_up"], np.float32)], axis=1))
    rowv = np.zeros((5, D), np.float32)
    rowv[0] = np.asarray(inputs["L0_ffn_gamma"], np.float32)
    rowv[1] = np.asarray(inputs["L0_attn_ln_scale"], np.float32)
    rowv[2] = np.asarray(inputs["L0_attn_ln_bias"], np.float32)
    rowv[3] = np.asarray(inputs["L0_attn_b_proj"], np.float32)
    rowv[4, 0:16] = np.arange(16, dtype=np.float32)
    ident = np.eye(P, dtype=np.float32)
    w_proj = np.ascontiguousarray(np.asarray(inputs["L0_attn_w_proj"], np.float32))
    shared = dict(fused32=fused32, wqT=wqT, keysT=keysT, wq_a=wq_a, wk_a=wk_a,
                  wv_a=wv_a, w_proj=w_proj, rowv=rowv, ident_in=ident)
    return shared


def kernel(**inputs):
    cfg = Cfg()
    z = np.asarray(inputs["L0_attn_z"], np.float32)
    ncore = z.shape[0]
    shared = _host_prep(inputs, cfg)
    in_maps = []
    for cidx in range(ncore):
        m = dict(shared)
        m["z_in"] = np.ascontiguousarray(z[cidx])
        in_maps.append(m)
    nc = build_nc(cfg)
    res = run_bass_kernel_spmd(nc, in_maps, core_ids=list(range(ncore)))
    out = np.stack([np.asarray(r["x_out"], np.float32) for r in res.results])
    return out


# version 07 (final)
# speedup vs baseline: 1.0872x; 1.0607x over previous
import math
import numpy as np
import ml_dtypes
import concourse.bass as bass
import concourse.mybir as mybir
from concourse.bass_utils import run_bass_kernel_spmd
from contextlib import ExitStack

F32 = mybir.dt.float32
BF16 = mybir.dt.bfloat16
U16 = mybir.dt.uint16
U32 = mybir.dt.uint32
U8 = mybir.dt.uint8
AF = mybir.ActivationFunctionType
OP = mybir.AluOpType
AX = mybir.AxisListType
bfdt = ml_dtypes.bfloat16

P = 128
LN_EPS = 1e-5


class Cfg:
    def __init__(self, T=2048, NEXP=65536, conv_chunks=64):
        self.T = T
        self.NT = T // 128
        self.NB = max(1, T // 512)
        self.TB = T // self.NB
        self.NTB = self.TB // 128
        self.NEXP = NEXP
        self.EW = 2048
        self.D = 1024
        self.DC = 8
        self.FH = 8
        self.NK = 256
        self.DK = 512
        self.DKC = 4
        self.PH = 16
        self.NP = 8
        self.conv_chunks = conv_chunks
        self.SLOTS = 128
        self.GW = 1
        self.GR = 6


class Emit:
    ENGS = ("sync", "scalar", "vector", "tensor", "gpsimd")

    def __init__(self, nc, block, ctx, sem_names):
        self.nc = nc
        self.block = block
        self.sems = {n: ctx.enter_context(nc.semaphore(n)) for n in sem_names}
        self.c = {n: 0 for n in sem_names}
        self.last_wait = {}
        self.pending = {e: [] for e in self.ENGS}

    def flush(self):
        for eng in self.ENGS:
            ops = self.pending[eng]
            if ops:
                getattr(self.block, eng)(lambda e, ops=ops: [f(e) for f in ops])
            self.pending[eng] = []

    CHAIN = {"vector": "dv", "scalar": "ac"}

    def op(self, eng, f, waits=(), inc=None, n=1):
        waits = list(waits)
        ch = self.CHAIN.get(eng)
        if ch is not None and f is not None:
            waits.append((ch, self.c[ch]))
        wl = []
        for s, v in waits:
            if v is None or v <= 0:
                continue
            k = (eng, s)
            if self.last_wait.get(k, -1) < v:
                wl.append((self.sems[s], v))
                self.last_wait[k] = v
        if inc is not None:
            sem = self.sems[inc]
            self.c[inc] += n
            def body(e, f=f, wl=wl, sem=sem, n=n):
                for s, v in wl:
                    e.wait_ge(s, v)
                f(e).then_inc(sem, n)
            self.pending[eng].append(body)
            return self.c[inc]
        else:
            def body(e, f=f, wl=wl):
                for s, v in wl:
                    e.wait_ge(s, v)
                if f is not None:
                    f(e)
            self.pending[eng].append(body)
            return None

    def fence(self):
        snap = [(s, v) for s, v in self.c.items() if v > 0]
        for eng in self.ENGS:
            self.op(eng, None, waits=snap)


def build_nc(cfg: Cfg):
    nc = bass.Bass("TRN2", target_bir_lowering=False)
    T, NT, NB, NTB, TB = cfg.T, cfg.NT, cfg.NB, cfg.NTB, cfg.TB
    D, DC, EW, FH, DKC, NK = cfg.D, cfg.DC, cfg.EW, cfg.FH, cfg.DKC, cfg.NK
    GW, GR, SLOTS = cfg.GW, cfg.GR, cfg.SLOTS
    NQB = T // 512

    z_in = nc.dram_tensor("z_in", [T, D], F32, kind="ExternalInput").ap()
    fused32 = nc.dram_tensor("fused32", [cfg.NEXP, EW], F32, kind="ExternalInput").ap()
    wqT_d = nc.dram_tensor("wqT", [D, 2 * FH * cfg.DK], F32, kind="ExternalInput").ap()
    keysT_d = nc.dram_tensor("keysT", [cfg.PH, cfg.DK, NK], F32, kind="ExternalInput").ap()
    wq_a = nc.dram_tensor("wq_a", [D, D], F32, kind="ExternalInput").ap()
    wk_a = nc.dram_tensor("wk_a", [D, D], F32, kind="ExternalInput").ap()
    wv_a = nc.dram_tensor("wv_a", [D, D], F32, kind="ExternalInput").ap()
    w_proj = nc.dram_tensor("w_proj", [D, D], F32, kind="ExternalInput").ap()
    rowv = nc.dram_tensor("rowv", [5, D], F32, kind="ExternalInput").ap()
    ident_in = nc.dram_tensor("ident_in", [P, P], F32, kind="ExternalInput").ap()

    dbg = {}
    if getattr(cfg, "debug", False):
        for nm, shp, dt in (("dbg_x1", [T, D], F32), ("dbg_x2", [T, D], F32),
                            ("dbg_eid", [P, NT, FH, 16], U32),
                            ("dbg_wgt", [P, NT, FH, 16], F32),
                            ("dbg_xn", [P, NT, D], BF16),
                            ("dbg_zn", [P, DC, T], BF16),
                            ("dbg_q", [P, T], BF16), ("dbg_k", [P, T], BF16),
                            ("dbg_o", [P, DC, T], BF16)):
            dbg[nm] = nc.dram_tensor(nm, shp, dt, kind="ExternalOutput").ap()
    fusedbf = nc.dram_tensor("fusedbf", [cfg.NEXP, EW], BF16, kind="Internal").ap()
    negm_dram = nc.dram_tensor("negm_dram", [2, T], BF16, kind="Internal").ap()
    x_out = nc.dram_tensor("x_out", [T, D], F32, kind="ExternalOutput").ap()

    ctx = ExitStack()
    with ctx:
        def sb(name, shape, dt):
            return ctx.enter_context(nc.sbuf_tensor(name, shape, dt))

        def psum(name, shape, dt=F32):
            return ctx.enter_context(nc.psum_tensor(name, shape, dt))

        X = sb("X", [P, NT, D], F32)
        gamma_b = sb("gamma_b", [P, D], F32)
        lns_b = sb("lns_b", [P, D], F32)
        lnb_b = sb("lnb_b", [P, D], F32)
        bproj_b = sb("bproj_b", [P, D], F32)
        ramp16 = sb("ramp16", [P, 16], F32)
        ident = sb("ident", [P, P], BF16)
        ones1 = sb("ones1", [1, P], BF16)
        eid = sb("eid", [P, NT, FH, 16], U32)
        wgt = sb("wgt", [P, NT, FH, 16], F32)
        junkb = sb("junkb", [P, 1024], BF16)
        stat = sb("stat", [P, 32], F32)
        kconst = sb("kconst", [P, 2], F32)

        ar1 = sb("ar1", [P, NT * D * 2], U8)
        xn_tok = ar1[:, :].bitcast(BF16).rearrange("p (a b) -> p a b", a=NT)
        zn_fm = ar1[:, :].bitcast(BF16).rearrange("p (a b) -> p a b", a=DC)

        AR2B = 75 * 1024
        ar2 = sb("ar2", [P, AR2B], U8)
        cursor = [0]

        def carve(shape, dt):
            esz = {BF16: 2, U16: 2, F32: 4, U32: 4, U8: 1}[dt]
            nb = int(np.prod(shape[1:])) * esz
            o = cursor[0]
            assert o + nb <= AR2B, f"arena2 overflow at {o + nb}"
            cursor[0] = (o + nb + 127) // 128 * 128
            v = ar2[:, o:o + nb].bitcast(dt)
            if len(shape) == 2:
                return v
            dims = dict(zip("ab", shape[1:-1]))
            pat = " ".join("ab"[: len(shape) - 2])
            return v.rearrange(f"p ({pat} z) -> p {pat} z", **dims)

        wqs = [carve([P, DC, 512], BF16) for _ in range(2)]
        keyss = [carve([P, DKC, NK], BF16) for _ in range(2)]
        xn_fm = carve([P, DC, TB], BF16)
        qT = carve([P, DKC, TB], BF16)
        sxsy = carve([P, NTB, 2 * FH * 16], F32)
        ixiy = carve([P, NTB, 2 * FH * 16], U16)
        scr256 = carve([P, 256], F32)
        grid = carve([P, 256], F32)
        gridm = carve([P, 256], F32)
        v16 = carve([P, 16], F32)
        p16 = carve([P, 16], U16)
        absel = carve([P, 32], U16)
        ohq = carve([P, 256], F32)
        usel = carve([P, 16], F32)
        wsel = carve([P, 16], F32)
        eidf = carve([P, 16], F32)
        ew16 = carve([P, 16], F32)
        ztmp = carve([P, D], F32)
        xupd = ztmp
        gbuf = carve([P, GR * GW, EW], BF16)
        hbuf = carve([P, SLOTS], F32)
        dga = [carve([P, P], F32) for _ in range(2)]
        diag2 = [carve([P, GW * P], BF16) for _ in range(2)]
        ffn_end = cursor[0]

        cursor[0] = 0
        O_fm = carve([P, DC, T], BF16)
        az = carve([P, D], F32)
        aupd = carve([P, D], F32)
        attn_base = cursor[0]
        qTp = carve([P, T], BF16)
        kTp = carve([P, T], BF16)
        vP = carve([P, NT, 132], BF16)
        kaug = carve([P, T], BF16)
        qaug = carve([P, T], BF16)
        expT = carve([P, 4, 512], BF16)
        wstr = [carve([P, DC, P], BF16) for _ in range(2)]
        negm2 = carve([P, 2, NT], BF16)
        m4 = carve([P, 8], F32)
        zi_sb = carve([P, 512], BF16)
        zrowf = carve([P, 512], F32)
        zrowb = carve([P, 512], BF16)
        oev = carve([P, 512], BF16)
        attn_end = cursor[0]
        cursor[0] = attn_base
        pwb = [carve([P, DC, 512], BF16) for _ in range(2)]
        proj_end = cursor[0]
        assert max(ffn_end, attn_end, proj_end) <= AR2B

        ps_tr = psum("ps_tr", [P, 1024], BF16)
        ps_q = [psum(f"ps_q{i}", [P, 512], F32) for i in range(2)]
        ps_s = [psum(f"ps_s{i}", [P, 512], F32) for i in range(2)]
        ps_acc = psum("ps_acc", [P, 1024], F32)
        ps_x1 = psum("ps_x1", [P, 512], F32)

        sems = ["ld", "st", "ag", "od", "nm", "cvt", "sw",
                "sw0", "sw1", "gt0", "gt1", "gt2", "gt3", "gt4", "gt5",
                "pe", "dv", "ac", "pl"]
        block = ctx.enter_context(nc.Block())
        E = Emit(nc, block, ctx, sems)
        c = E.c
        st8 = {}

        E.op("sync", lambda e: e.dma_start(
            out=X[:], in_=z_in.rearrange("(t p) d -> p t d", p=P)), inc="ld", n=16)
        for i, buf in enumerate((gamma_b, lns_b, lnb_b, bproj_b)):
            E.op("sync", lambda e, i=i, buf=buf: e.dma_start(
                out=buf[:], in_=rowv[i:i + 1, :].to_broadcast([P, D])), inc="ld", n=16)
        E.op("sync", lambda e: e.dma_start(
            out=ramp16[:], in_=rowv[4:5, 0:16].to_broadcast([P, 16])), inc="ld", n=16)
        ld0 = c["ld"]
        E.op("gpsimd", lambda e: e.dma_start(out=ident[:], in_=ident_in), inc="sw", n=16)
        sw0_c = c["sw"]
        E.op("gpsimd", lambda e: e.memset(ones1[:], 1.0), inc="pl", n=1)
        pl_ones = c["pl"]
        E.op("vector", lambda e: e.memset(kconst[:, 0:1], LN_EPS), inc="dv")
        kc_dv = E.op("vector", lambda e: e.memset(kconst[:, 1:2], 0.125), inc="dv")
        rows_per = cfg.NEXP // cfg.conv_chunks
        cvt_done = 16 * cfg.conv_chunks
        conv_emitted = [0]

        def emit_conv_chunks(k):
            while conv_emitted[0] < cfg.conv_chunks and k > 0:
                ch = conv_emitted[0]
                E.op("gpsimd", lambda e, ch=ch: e.dma_start(
                    out=fusedbf[ch * rows_per:(ch + 1) * rows_per, :],
                    in_=fused32[ch * rows_per:(ch + 1) * rows_per, :]),
                    inc="cvt", n=16)
                conv_emitted[0] += 1
                k -= 1

        state = {
            "ln_src": {t: [("ld", ld0)] for t in range(NT)},
            "bank": {},
        }

        def bank_w(name):
            return state["bank"].get(name, [])

        def bank_r(name, sem, val):
            state["bank"][name] = [(sem, val)]

        def emit_ln(t, zdst, extra=()):
            w = list(state["ln_src"][t])
            E.op("vector", lambda e, t=t: e.bn_stats(
                out=stat[:, 0:6], in_=X[:, t, 0:512]), waits=w, inc="dv")
            E.op("vector", lambda e, t=t: e.bn_stats(
                out=stat[:, 6:12], in_=X[:, t, 512:1024]), inc="dv")
            d1 = E.op("vector", lambda e: e.bn_aggr(
                out=stat[:, 12:14], in_=stat[:, 0:12]),
                waits=[("dv", c["dv"])], inc="dv")
            a1 = E.op("scalar", lambda e: e.activation(
                out=stat[:, 14:15], in_=stat[:, 13:14], func=AF.Sqrt,
                bias=kconst[:, 0:1]),
                waits=[("dv", d1), ("dv", kc_dv)], inc="ac")
            d2 = E.op("vector", lambda e: e.reciprocal(
                out=stat[:, 15:16], in_=stat[:, 14:15]), waits=[("ac", a1)], inc="dv")
            d3 = E.op("vector", lambda e, t=t, zdst=zdst: e.tensor_scalar(
                out=zdst[:], in0=X[:, t, :], scalar1=stat[:, 12:13],
                scalar2=stat[:, 15:16], op0=OP.subtract, op1=OP.mult),
                waits=[("dv", d2)] + list(extra), inc="dv")
            return d3

        def emit_ffn(call):
            xn_done = {}
            for t in range(NT):
                dz = emit_ln(t, ztmp, extra=[("ac", state.get("ztmp_free", 0))])
                a1 = E.op("scalar", lambda e: e.activation(
                    out=junkb[:, 0:D], in_=ztmp[:], func=AF.Square,
                    accum_out=stat[:, 16:17]), waits=[("dv", dz)], inc="ac")
                state["ztmp_free"] = a1
                a2 = E.op("scalar", lambda e: e.activation(
                    out=stat[:, 17:18], in_=stat[:, 16:17], func=AF.Sqrt),
                    waits=[("ac", a1)], inc="ac")
                d1 = E.op("vector", lambda e: e.tensor_scalar(
                    out=stat[:, 18:19], in0=stat[:, 17:18], scalar1=1e-12,
                    scalar2=None, op0=OP.max, op1=OP.bypass),
                    waits=[("ac", a2)], inc="dv")
                d2 = E.op("vector", lambda e: e.reciprocal(
                    out=stat[:, 19:20], in_=stat[:, 18:19]), waits=[("dv", d1)], inc="dv")
                d3 = E.op("vector", lambda e: e.tensor_scalar(
                    out=stat[:, 20:21], in0=stat[:, 19:20],
                    scalar1=float(math.sqrt(D)), scalar2=None,
                    op0=OP.mult, op1=OP.bypass), waits=[("dv", d2)], inc="dv")
                d4 = E.op("vector", lambda e, t=t: e.scalar_tensor_tensor(
                    out=xn_tok[:, t, :], in0=ztmp[:], scalar=stat[:, 20:21],
                    in1=gamma_b[:], op0=OP.mult, op1=OP.mult),
                    waits=[("dv", d3)], inc="dv")
                xn_done[t] = d4

            def emit_bc(b):
                for tt in range(NTB):
                    t = b * NTB + tt
                    for cc in range(DC):
                        E.op("tensor", lambda e, t=t, cc=cc: e.transpose(
                            ps_tr[:, cc * 128:(cc + 1) * 128],
                            xn_tok[:, t, cc * 128:(cc + 1) * 128], ident[:]),
                            waits=[("dv", xn_done[t]), ("sw", sw0_c)] + bank_w("tr"),
                            inc="pe")
                    ev = E.op("vector", lambda e, tt=tt: e.tensor_copy(
                        out=xn_fm[:, :, tt * 128:(tt + 1) * 128],
                        in_=ps_tr[:].rearrange("p (a z) -> p a z", a=DC)),
                        waits=[("pe", c["pe"])], inc="dv")
                    bank_r("tr", "dv", ev)
                xn_fm_done = c["dv"]

                eid_done = {}
                for ph in range(cfg.PH):
                    par = ph % 2
                    gsem = "sw0" if par == 0 else "sw1"
                    E.op("gpsimd", lambda e, ph=ph, par=par: e.dma_start(
                        out=wqs[par][:, :, :],
                        in_=wqT_d[:, ph * 512:(ph + 1) * 512]
                            .rearrange("(a p) m -> p a m", p=P)),
                        waits=[("pe", state.get(("wqfree", par), 0))],
                        inc=gsem, n=16)
                    E.op("gpsimd", lambda e, ph=ph, par=par: e.dma_start(
                        out=keyss[par][:, :, :],
                        in_=keysT_d[ph].rearrange("(a p) m -> p a m", p=P)),
                        inc=gsem, n=16)
                    gval = c[gsem]
                    emit_conv_chunks(2)

                    qdone = [0] * DKC
                    for dkc in range(DKC):
                        qp = dkc % 2
                        for kc in range(DC):
                            E.op("tensor", lambda e, dkc=dkc, kc=kc, par=par, qp=qp:
                                 e.matmul(ps_q[qp][:, 0:TB],
                                          wqs[par][:, kc, dkc * 128:(dkc + 1) * 128],
                                          xn_fm[:, kc, :],
                                          start=(kc == 0), stop=(kc == DC - 1)),
                                 waits=[(gsem, gval), ("dv", xn_fm_done)]
                                 + bank_w(f"q{qp}"), inc="pe")
                        ev = E.op("scalar", lambda e, dkc=dkc, qp=qp: e.activation(
                            out=qT[:, dkc, :], in_=ps_q[qp][:, 0:TB], func=AF.Copy),
                            waits=[("pe", c["pe"])], inc="ac")
                        bank_r(f"q{qp}", "ac", ev)
                        qdone[dkc] = ev

                    p_, h_ = ph // FH, ph % FH
                    for tt in range(NTB):
                        sp = tt % 2
                        for dkc in range(DKC):
                            pe_l = E.op("tensor",
                                 lambda e, tt=tt, dkc=dkc, par=par, sp=sp:
                                 e.matmul(ps_s[sp][:, 0:NK],
                                          qT[:, dkc, tt * 128:(tt + 1) * 128],
                                          keyss[par][:, dkc, :],
                                          start=(dkc == 0), stop=(dkc == DKC - 1)),
                                 waits=[("ac", qdone[dkc]), (gsem, gval)]
                                 + bank_w(f"s{sp}"), inc="pe")
                        o1 = (p_ * FH + h_) * 16
                        d1 = E.op("vector", lambda e, tt=tt, sp=sp, o1=o1:
                             e.max(out=sxsy[:, tt, o1:o1 + 8], in_=ps_s[sp][:, 0:NK]),
                             waits=[("pe", c["pe"])], inc="dv")
                        d2 = E.op("vector", lambda e, tt=tt, sp=sp, o1=o1:
                             e.max_index(out=ixiy[:, tt, o1:o1 + 8],
                                         in_max=sxsy[:, tt, o1:o1 + 8],
                                         in_values=ps_s[sp][:, 0:NK]),
                             waits=[("dv", d1)], inc="dv")
                        d3 = E.op("vector", lambda e, tt=tt, sp=sp, o1=o1:
                             e.match_replace(out=scr256[:],
                                             in_to_replace=sxsy[:, tt, o1:o1 + 8],
                                             in_values=ps_s[sp][:, 0:NK],
                                             imm_value=-1e30),
                             waits=[("dv", d2)], inc="dv")
                        bank_r(f"s{sp}", "dv", d3)
                        d4 = E.op("vector", lambda e, tt=tt, o1=o1:
                             e.max(out=sxsy[:, tt, o1 + 8:o1 + 16], in_=scr256[:]),
                             waits=[("dv", d3)], inc="dv")
                        E.op("vector", lambda e, tt=tt, o1=o1:
                             e.max_index(out=ixiy[:, tt, o1 + 8:o1 + 16],
                                         in_max=sxsy[:, tt, o1 + 8:o1 + 16],
                                         in_values=scr256[:]),
                             waits=[("dv", d4)], inc="dv")
                    state[("wqfree", par)] = c["pe"]

                stage1_done = c["dv"]
                for tt in range(NTB):
                    t = b * NTB + tt
                    for h_ in range(FH):
                        ox = (0 * FH + h_) * 16
                        oy = (1 * FH + h_) * 16
                        g1 = E.op("vector", lambda e, tt=tt, ox=ox, oy=oy:
                             e.tensor_tensor(
                                 out=grid[:],
                                 in0=sxsy[:, tt, ox:ox + 16].unsqueeze(2)
                                     .to_broadcast([P, 16, 16]),
                                 in1=sxsy[:, tt, oy:oy + 16].unsqueeze(1)
                                     .to_broadcast([P, 16, 16]),
                                 op=OP.add),
                             waits=[("dv", stage1_done)], inc="dv")
                        g2 = E.op("vector", lambda e: e.max(
                            out=v16[:, 0:8], in_=grid[:]), waits=[("dv", g1)], inc="dv")
                        g3 = E.op("vector", lambda e: e.max_index(
                            out=p16[:, 0:8], in_max=v16[:, 0:8], in_values=grid[:]),
                            waits=[("dv", g2)], inc="dv")
                        g4 = E.op("vector", lambda e: e.match_replace(
                            out=gridm[:], in_to_replace=v16[:, 0:8],
                            in_values=grid[:], imm_value=-1e30),
                            waits=[("dv", g3)], inc="dv")
                        g5 = E.op("vector", lambda e: e.max(
                            out=v16[:, 8:16], in_=gridm[:]), waits=[("dv", g4)], inc="dv")
                        g6 = E.op("vector", lambda e: e.max_index(
                            out=p16[:, 8:16], in_max=v16[:, 8:16], in_values=gridm[:]),
                            waits=[("dv", g5)], inc="dv")
                        g7 = E.op("vector", lambda e: e.tensor_scalar(
                            out=absel[:, 0:16], in0=p16[:], scalar1=4, scalar2=None,
                            op0=OP.logical_shift_right, op1=OP.bypass),
                            waits=[("dv", g6)], inc="dv")
                        g8 = E.op("vector", lambda e: e.tensor_scalar(
                            out=absel[:, 16:32], in0=p16[:], scalar1=15, scalar2=None,
                            op0=OP.bitwise_and, op1=OP.bypass), inc="dv")
                        g9 = E.op("vector", lambda e: e.tensor_tensor(
                            out=ohq[:],
                            in0=absel[:, 0:16].unsqueeze(2).to_broadcast([P, 16, 16]),
                            in1=ramp16[:].unsqueeze(1).to_broadcast([P, 16, 16]),
                            op=OP.is_equal), waits=[("dv", g7)], inc="dv")
                        g10 = E.op("vector", lambda e, tt=tt, ox=ox: e.tensor_tensor(
                            out=ohq[:], in0=ohq[:],
                            in1=ixiy[:, tt, ox:ox + 16].unsqueeze(1)
                                .to_broadcast([P, 16, 16]),
                            op=OP.mult), waits=[("dv", g9)], inc="dv")
                        g11 = E.op("vector", lambda e: e.tensor_reduce(
                            out=usel[:], in_=ohq[:].rearrange("p (a z) -> p a z", a=16),
                            axis=AX.X, op=OP.add), waits=[("dv", g10)], inc="dv")
                        g12 = E.op("vector", lambda e: e.tensor_tensor(
                            out=ohq[:],
                            in0=absel[:, 16:32].unsqueeze(2).to_broadcast([P, 16, 16]),
                            in1=ramp16[:].unsqueeze(1).to_broadcast([P, 16, 16]),
                            op=OP.is_equal),
                            waits=[("dv", g11), ("dv", g8)], inc="dv")
                        g13 = E.op("vector", lambda e, tt=tt, oy=oy: e.tensor_tensor(
                            out=ohq[:], in0=ohq[:],
                            in1=ixiy[:, tt, oy:oy + 16].unsqueeze(1)
                                .to_broadcast([P, 16, 16]),
                            op=OP.mult), waits=[("dv", g12)], inc="dv")
                        g14 = E.op("vector", lambda e: e.tensor_reduce(
                            out=wsel[:], in_=ohq[:].rearrange("p (a z) -> p a z", a=16),
                            axis=AX.X, op=OP.add), waits=[("dv", g13)], inc="dv")
                        g15 = E.op("vector", lambda e: e.scalar_tensor_tensor(
                            out=eidf[:], in0=usel[:], scalar=256.0, in1=wsel[:],
                            op0=OP.mult, op1=OP.add), waits=[("dv", g14)], inc="dv")
                        g16 = E.op("vector", lambda e, t=t, h_=h_: e.tensor_copy(
                            out=eid[:, t, h_, :], in_=eidf[:]),
                            waits=[("dv", g15)] + [
                                (f"gt{r}", st8.get(("eidfree", t, h_, r), 0))
                                for r in range(GR)],
                            inc="dv")
                        g17 = E.op("vector", lambda e: e.tensor_scalar(
                            out=stat[:, 21:22], in0=v16[:, 0:1], scalar1=-1.0,
                            scalar2=None, op0=OP.mult, op1=OP.bypass),
                            waits=[("dv", g6)], inc="dv")
                        a1 = E.op("scalar", lambda e: e.activation(
                            out=ew16[:], in_=v16[:], func=AF.Exp, bias=stat[:, 21:22],
                            accum_out=stat[:, 22:23]), waits=[("dv", g17)], inc="ac")
                        dz = E.op("vector", lambda e: e.reciprocal(
                            out=stat[:, 23:24], in_=stat[:, 22:23]),
                            waits=[("ac", a1)], inc="dv")
                        g18 = E.op("vector", lambda e, t=t, h_=h_: e.tensor_scalar(
                            out=wgt[:, t, h_, :], in0=ew16[:],
                            scalar1=stat[:, 23:24], scalar2=None,
                            op0=OP.mult, op1=OP.bypass), waits=[("dv", dz)], inc="dv")
                        eid_done[(t, h_)] = max(g16, g18)
                return eid_done

            def emit_d(b, eid_done):
                for tt in range(NTB):
                    t = b * NTB + tt
                    edone = max(eid_done[(t, h_)] for h_ in range(FH))
                    nw = SLOTS // GW
                    wflat = wgt[:, t, :, :].rearrange("p a z -> p (a z)")

                    def emit_tail(wv, t=t, wflat=wflat, edone=edone):
                        assert GW == 1
                        pp = wv % 2
                        s0 = wv * GW
                        ring = wv % GR
                        dA = E.op("scalar", lambda e, s0=s0, pp=pp:
                             e.activation(
                                 out=dga[pp][:], in_=ident[:], func=AF.Gelu,
                                 scale=hbuf[:, s0:s0 + 1]),
                             waits=[("dv", state[("dots_w", wv)]),
                                    ("pe", state.get(("diagfree", pp), 0))],
                             inc="ac")
                        state[("hfree", wv)] = dA
                        d6 = E.op("scalar", lambda e, s0=s0, pp=pp, wflat=wflat:
                             e.activation(
                                 out=diag2[pp][:], in_=dga[pp][:], func=AF.Copy,
                                 scale=wflat[:, s0:s0 + 1]),
                             waits=[("dv", edone)], inc="ac")
                        d6w = ("ac", d6)
                        for j in range(GW):
                            s = wv * GW + j
                            for half in range(2):
                                E.op("tensor",
                                     lambda e, ring=ring, j=j, s=s, half=half, pp=pp:
                                     e.matmul(ps_acc[:, half * 512:(half + 1) * 512],
                                              diag2[pp][:, j * P:(j + 1) * P],
                                              gbuf[:, ring * GW + j,
                                                   D + half * 512:D + (half + 1) * 512],
                                              start=(s == 0), stop=(s == SLOTS - 1),
                                              skip_group_check=True),
                                     waits=[d6w] +
                                     (bank_w("acc") if s == 0 and half == 0 else []),
                                     inc="pe")
                        state[("gslot", ring)] = (state[("dots", ring)], c["pe"])
                        state[("diagfree", pp)] = c["pe"]

                    for wv in range(nw):
                        ring = wv % GR
                        gsem = f"gt{ring}"
                        war = state.get(("gslot", ring))
                        waits = [("cvt", cvt_done), ("dv", edone)]
                        if war:
                            waits += [("dv", war[0]), ("pe", war[1])]
                        for j in range(GW):
                            s = wv * GW + j
                            h_, k_ = s // 16, s % 16
                            E.op("gpsimd", lambda e, t=t, h_=h_, k_=k_, ring=ring, j=j:
                                 e.indirect_dma_start(
                                     out=gbuf[:, ring * GW + j, :], out_offset=None,
                                     in_=fusedbf,
                                     in_offset=bass.IndirectOffsetOnAxis(
                                         ap=eid[:, t, h_, k_:k_ + 1], axis=0)),
                                 waits=waits, inc=gsem, n=16)
                            waits = []
                            st8[("eidfree", t, h_, ring)] = c[gsem]
                        gval = c[gsem]
                        dots = 0
                        for j in range(GW):
                            s = wv * GW + j
                            dots = E.op("vector", lambda e, t=t, ring=ring, j=j, s=s:
                                 e.scalar_tensor_tensor(
                                     out=junkb[:, 0:D],
                                     in0=gbuf[:, ring * GW + j, 0:D],
                                     scalar=1.0, in1=xn_tok[:, t, :],
                                     op0=OP.mult, op1=OP.mult,
                                     accum_out=hbuf[:, s:s + 1]),
                                 waits=[(gsem, gval), ("dv", xn_done[t]),
                                        ("ac", state.get(("hfree", wv), 0))],
                                 inc="dv")
                        state[("dots", ring)] = dots
                        state[("dots_w", wv)] = dots
                        if wv > 0:
                            emit_tail(wv - 1)
                    emit_tail(nw - 1)
                    up_done = c["pe"]
                    d7 = E.op("vector", lambda e, t=t: e.scalar_tensor_tensor(
                        out=xupd[:], in0=ps_acc[:], scalar=0.5, in1=X[:, t, :],
                        op0=OP.mult, op1=OP.add), waits=[("pe", up_done)], inc="dv")
                    bank_r("acc", "dv", d7)
                    d8 = E.op("vector", lambda e, t=t: e.tensor_copy(
                        out=X[:, t, :], in_=xupd[:]), waits=[("dv", d7)], inc="dv")
                    state["ln_src"][t] = [("dv", d8)]

            ed = [None] * NB
            if call == 0:
                for b in range(NB):
                    ed[b] = emit_bc(b)
                for b in range(NB):
                    emit_d(b, ed[b])
            else:
                ed[0] = emit_bc(0)
                for b in range(NB):
                    if b + 1 < NB:
                        ed[b + 1] = emit_bc(b + 1)
                    emit_d(b, ed[b])

        def emit_attn():
            zn_done = 0
            for t in range(NT):
                dz = emit_ln(t, az)
                d1 = E.op("vector", lambda e, t=t: e.tensor_tensor(
                    out=aupd[:], in0=X[:, t, :], in1=az[:], op=OP.add),
                    waits=[("dv", dz)], inc="dv")
                d2 = E.op("vector", lambda e, t=t: e.tensor_copy(
                    out=X[:, t, :], in_=aupd[:]), waits=[("dv", d1)], inc="dv")
                state["ln_src"][t] = [("dv", d2)]
                E.op("vector", lambda e: e.bn_stats(out=stat[:, 0:6], in_=az[:, 0:512]),
                     waits=[("dv", dz)], inc="dv")
                E.op("vector", lambda e: e.bn_stats(out=stat[:, 6:12], in_=az[:, 512:1024]),
                     inc="dv")
                d3 = E.op("vector", lambda e: e.bn_aggr(
                    out=stat[:, 12:14], in_=stat[:, 0:12]),
                    waits=[("dv", c["dv"])], inc="dv")
                a1 = E.op("scalar", lambda e: e.activation(
                    out=stat[:, 14:15], in_=stat[:, 13:14], func=AF.Sqrt,
                    bias=kconst[:, 0:1]),
                    waits=[("dv", d3), ("dv", kc_dv)], inc="ac")
                d4 = E.op("vector", lambda e: e.reciprocal(
                    out=stat[:, 15:16], in_=stat[:, 14:15]), waits=[("ac", a1)], inc="dv")
                d5 = E.op("vector", lambda e: e.tensor_scalar(
                    out=aupd[:], in0=az[:], scalar1=stat[:, 12:13],
                    scalar2=stat[:, 15:16], op0=OP.subtract, op1=OP.mult),
                    waits=[("dv", d4), ("dv", d2)], inc="dv")
                d6 = E.op("vector", lambda e: e.tensor_tensor(
                    out=aupd[:], in0=aupd[:], in1=lns_b[:], op=OP.mult),
                    waits=[("dv", d5)], inc="dv")
                d7 = E.op("vector", lambda e: e.tensor_tensor(
                    out=junkb[:, 0:D], in0=aupd[:], in1=lnb_b[:], op=OP.add),
                    waits=[("dv", d6), ("pe", state.get("znj_free", 0))], inc="dv")
                for cc in range(DC):
                    E.op("tensor", lambda e, cc=cc: e.transpose(
                        ps_tr[:, cc * 128:(cc + 1) * 128],
                        junkb[:, cc * 128:(cc + 1) * 128], ident[:]),
                        waits=[("dv", d7), ("sw", sw0_c)] + bank_w("tr"), inc="pe")
                state["znj_free"] = c["pe"]
                ev = E.op("vector", lambda e, t=t: e.tensor_copy(
                    out=zn_fm[:, :, t * 128:(t + 1) * 128],
                    in_=ps_tr[:].rearrange("p (a z) -> p a z", a=DC)),
                    waits=[("pe", c["pe"])], inc="dv")
                bank_r("tr", "dv", ev)
                zn_done = ev

            for hp in range(cfg.NP):
                par = hp % 2
                gsem = "sw0" if par == 0 else "sw1"
                qk_ev = {}
                for which, (wt, buf, scl) in enumerate(
                        ((wq_a, qTp, "q"), (wk_a, kTp, None))):
                    E.op("gpsimd", lambda e, wt=wt, hp=hp, par=par: e.dma_start(
                        out=wstr[par][:, :, :],
                        in_=wt[:, hp * 128:(hp + 1) * 128]
                            .rearrange("(a p) m -> p a m", p=P)),
                        waits=[("pe", c["pe"])], inc=gsem, n=16)
                    gval = c[gsem]
                    for tb in range(NQB):
                        qp = tb % 2
                        for kc in range(DC):
                            E.op("tensor", lambda e, tb=tb, kc=kc, par=par, qp=qp:
                                 e.matmul(ps_q[qp][:],
                                          wstr[par][:, kc, :],
                                          zn_fm[:, kc, tb * 512:(tb + 1) * 512],
                                          start=(kc == 0), stop=(kc == DC - 1)),
                                 waits=[(gsem, gval), ("dv", zn_done)]
                                 + bank_w(f"q{qp}"), inc="pe")
                        if scl == "q":
                            ev = E.op("scalar", lambda e, tb=tb, buf=buf, qp=qp:
                                      e.activation(out=buf[:, tb * 512:(tb + 1) * 512],
                                                   in_=ps_q[qp][:], func=AF.Copy,
                                                   scale=kconst[:, 1:2]),
                                      waits=[("pe", c["pe"]), ("dv", kc_dv)], inc="ac")
                        else:
                            ev = E.op("scalar", lambda e, tb=tb, buf=buf, qp=qp:
                                      e.activation(out=buf[:, tb * 512:(tb + 1) * 512],
                                                   in_=ps_q[qp][:], func=AF.Copy),
                                      waits=[("pe", c["pe"])], inc="ac")
                        bank_r(f"q{qp}", "ac", ev)
                    state[("wsfree", par)] = c["pe"]
                    qk_ev[which] = c["ac"]
                agw = [("ac", qk_ev[0]), ("ac", qk_ev[1]),
                       ("pe", state.get("augfree", 0))]
                E.op("sync", lambda e: e.dma_start(
                    out=qaug[0:64, :], in_=qTp[0:64, :]), waits=agw, inc="ag", n=16)
                E.op("sync", lambda e: e.dma_start(
                    out=kaug[0:64, :], in_=kTp[0:64, :]), inc="ag", n=16)
                aug_ld = c["ag"]
                ko = E.op("vector", lambda e: e.memset(kaug[64:65, :], 1.0),
                          waits=[("ag", aug_ld), ("pe", state.get("augfree", 0))],
                          inc="dv")

                sbanks = {0: ("q0", ps_q[0]), 1: ("q1", ps_q[1])}
                sbanks2 = {0: ("s0", ps_s[0]), 1: ("s1", ps_s[1])}
                for t in range(NT):
                    rm_last = {}
                    for tkb in range(NQB):
                        for side in range(2):
                            bname, pss = (sbanks if tkb % 2 == 0 else sbanks2)[side]
                            pv = E.op("tensor",
                                 lambda e, t=t, tkb=tkb, side=side, pss=pss:
                                 e.matmul(pss[:],
                                          qTp[side * 64:(side + 1) * 64,
                                              t * 128:(t + 1) * 128],
                                          kTp[side * 64:(side + 1) * 64,
                                              tkb * 512:(tkb + 1) * 512],
                                          start=True, stop=True,
                                          tile_position=(side * 64, 0)),
                                 waits=[("ac", qk_ev[1]), ("ag", aug_ld)]
                                 + bank_w(bname), inc="pe")
                            rm = E.op("vector",
                                 lambda e, t=t, tkb=tkb, side=side, pss=pss:
                                 e.tensor_scalar(
                                     out=junkb[:, 512:1024], in0=pss[:],
                                     scalar1=1.0, scalar2=-1e30,
                                     op0=OP.mult, op1=OP.max,
                                     accum_out=m4[:, side * 4 + tkb:
                                                  side * 4 + tkb + 1]),
                                 waits=[("pe", pv)], inc="dv")
                            bank_r(bname, "dv", rm)
                            rm_last[side] = rm
                    for side in range(2):
                        dmx = E.op("vector", lambda e, side=side: e.tensor_reduce(
                            out=stat[:, 24:25],
                            in_=m4[:, side * 4:side * 4 + NQB],
                            axis=AX.X, op=OP.max),
                            waits=[("dv", rm_last[side])], inc="dv")
                        E.op("vector", lambda e, t=t, side=side: e.tensor_scalar(
                            out=negm2[:, side, t:t + 1], in0=stat[:, 24:25],
                            scalar1=-1.0, scalar2=None, op0=OP.mult, op1=OP.bypass),
                            waits=[("dv", dmx), ("nm", state.get("negm_free", 0))],
                            inc="dv")
                negm_dv = c["dv"]
                for side in range(2):
                    E.op("sync", lambda e, side=side: e.dma_start(
                        out=negm_dram[side].rearrange("(t p) -> p t", p=P),
                        in_=negm2[:, side, :],
                        allow_slow_non_contiguous=True),
                        waits=[("dv", negm_dv)], inc="nm", n=16)
                nm_st = c["nm"]
                state["negm_free"] = nm_st

                E.op("gpsimd", lambda e, hp=hp, par=par: e.dma_start(
                    out=wstr[par][:, :, :],
                    in_=wv_a[:, hp * 128:(hp + 1) * 128]
                        .rearrange("(a p) m -> p a m", p=P)),
                    waits=[("pe", c["pe"])], inc=gsem, n=16)
                gval = c[gsem]
                dvo = E.op("vector", lambda e: e.memset(vP[:, :, 64:65], 1.0),
                           waits=[("pe", state.get("pvfree", 0))], inc="dv")
                dvo = E.op("vector", lambda e: e.memset(vP[:, :, 130:131], 1.0),
                           inc="dv")
                for t in range(NT):
                    qp = t % 2
                    for kc in range(DC):
                        E.op("tensor", lambda e, t=t, kc=kc, par=par, qp=qp:
                             e.matmul(ps_q[qp][:, 0:128],
                                      zn_fm[:, kc, t * 128:(t + 1) * 128],
                                      wstr[par][:, kc, :],
                                      start=(kc == 0), stop=(kc == DC - 1)),
                             waits=[(gsem, gval), ("dv", zn_done), ("dv", dvo)]
                             + bank_w(f"q{qp}"), inc="pe")
                    e1 = E.op("vector", lambda e, t=t, qp=qp: e.tensor_copy(
                        out=vP[:, t, 0:64], in_=ps_q[qp][:, 0:64]),
                        waits=[("pe", c["pe"])], inc="dv")
                    e2 = E.op("vector", lambda e, t=t, qp=qp: e.tensor_copy(
                        out=vP[:, t, 66:130], in_=ps_q[qp][:, 64:128]), inc="dv")
                    bank_r(f"q{qp}", "dv", e2)
                v_done = c["dv"]
                state[("wsfree", par)] = c["pe"]

                for side in range(2):
                    if side == 1:
                        agw2 = [("pe", c["pe"])]
                        E.op("sync", lambda e: e.dma_start(
                            out=qaug[0:64, :], in_=qTp[64:128, :]),
                            waits=agw2, inc="ag", n=16)
                        E.op("sync", lambda e: e.dma_start(
                            out=kaug[0:64, :], in_=kTp[64:128, :]), inc="ag", n=16)
                        aug_ld2 = c["ag"]
                        ko = E.op("vector", lambda e: e.memset(kaug[64:65, :], 1.0),
                                  waits=[("ag", aug_ld2)], inc="dv")
                    nrow = E.op("sync", lambda e, side=side: e.dma_start(
                        out=qaug[64:65, :], in_=negm_dram[side:side + 1, :]),
                        waits=[("nm", nm_st),
                               ("pe", c["pe"] if side == 1
                                else state.get("augfree", 0)),
                               ("ag", c["ag"])], inc="nm", n=16)
                    nm_row = c["nm"]
                    vofs = 0 if side == 0 else 66
                    h = 2 * hp + side
                    chunk, half = hp, side
                    for qb in range(NQB):
                        aes = {}
                        pvpar = (side * NQB + qb) % 2
                        pvps = ps_x1 if pvpar == 0 else ps_s[0]
                        pvbn = "x1" if pvpar == 0 else "s0"

                        def emit_pv(tkt, vofs=vofs, pvps=pvps, pvbn=pvbn):
                            er = tkt % 4
                            pv2 = E.op("tensor", lambda e, tkt=tkt, er=er,
                                       vofs=vofs, pvps=pvps:
                                 e.matmul(pvps[0:65, :],
                                          vP[:, tkt, vofs:vofs + 65],
                                          expT[:, er, :],
                                          start=(tkt == 0), stop=(tkt == NT - 1),
                                          skip_group_check=True),
                                 waits=[("ac", aes[tkt]), ("dv", v_done)]
                                 + (bank_w(pvbn) if tkt == 0 else []), inc="pe")
                            state[("pvc", er)] = pv2
                            return pv2

                        pvdone = 0
                        for tkt in range(NT):
                            er = tkt % 4
                            stp = tkt % 2
                            pv = E.op("tensor", lambda e, tkt=tkt, qb=qb, stp=stp:
                                 e.matmul(ps_acc[:, stp * 512:(stp + 1) * 512],
                                          kaug[0:65, tkt * 128:(tkt + 1) * 128],
                                          qaug[0:65, qb * 512:(qb + 1) * 512],
                                          start=True, stop=True),
                                 waits=[("nm", nm_row), ("dv", ko)]
                                 + bank_w(f"a{stp}"), inc="pe")
                            ae = E.op("scalar", lambda e, er=er, stp=stp:
                                 e.activation(out=expT[:, er, :],
                                              in_=ps_acc[:, stp * 512:(stp + 1) * 512],
                                              func=AF.Exp),
                                 waits=[("pe", pv), ("pe", state.get(("pvc", er), 0))],
                                 inc="ac")
                            bank_r(f"a{stp}", "ac", ae)
                            aes[tkt] = ae
                            if tkt >= 1:
                                pvdone = emit_pv(tkt - 1)
                        pvdone = emit_pv(NT - 1)
                        r1 = E.op("vector", lambda e, pvps=pvps: e.reciprocal(
                            out=zrowf[0:1, :], in_=pvps[64:65, :]),
                            waits=[("pe", pvdone)], inc="dv")
                        r2 = E.op("vector", lambda e: e.tensor_copy(
                            out=zrowb[0:1, :], in_=zrowf[0:1, :]),
                            waits=[("dv", r1)], inc="dv")
                        bz = E.op("tensor", lambda e: e.matmul(
                            ps_q[0][0:64, :], ones1[:, 0:64], zrowb[0:1, :],
                            start=True, stop=True),
                            waits=[("dv", r2), ("pl", pl_ones)]
                            + bank_w("q0"), inc="pe")
                        ze = E.op("vector", lambda e: e.tensor_copy(
                            out=zi_sb[0:64, :], in_=ps_q[0][0:64, :]),
                            waits=[("pe", bz)], inc="dv")
                        bank_r("q0", "dv", ze)
                        oe = E.op("vector", lambda e, pvps=pvps: e.tensor_tensor(
                            out=oev[0:64, :], in0=pvps[0:64, :],
                            in1=zi_sb[0:64, :], op=OP.mult),
                            waits=[("dv", ze), ("od", state.get("oev_free", 0))],
                            inc="dv")
                        bank_r(pvbn, "dv", oe)
                        E.op("sync", lambda e, qb=qb, chunk=chunk, half=half:
                             e.dma_start(
                                 out=O_fm[half * 64:(half + 1) * 64, chunk,
                                          qb * 512:(qb + 1) * 512],
                                 in_=oev[0:64, :]),
                             waits=[("dv", oe)], inc="od", n=16)
                        state["oev_free"] = c["od"]
                state["augfree"] = c["pe"]
                state["pvfree"] = c["pe"]
            o_all = c["od"]

            for half in range(2):
                E.op("gpsimd", lambda e, half=half: e.dma_start(
                    out=pwb[half][:, :, :],
                    in_=w_proj[:, half * 512:(half + 1) * 512]
                        .rearrange("(a p) m -> p a m", p=P)),
                    waits=[("pe", c["pe"]), ("dv", c["dv"]), ("od", c["od"]),
                           ("ac", c["ac"]), ("nm", c["nm"]), ("ag", c["ag"])],
                    inc="sw", n=16)
            pw_ld = c["sw"]
            for t in range(NT):
                xw = []
                for half in range(2):
                    hp_ = half
                    bname = "q0" if half == 0 else "q1"
                    pss = ps_q[half]
                    for kc in range(DC):
                        E.op("tensor", lambda e, t=t, kc=kc, half=half, pss=pss:
                             e.matmul(pss[:],
                                      O_fm[:, kc, t * 128:(t + 1) * 128],
                                      pwb[half][:, kc, :],
                                      start=(kc == 0), stop=(kc == DC - 1)),
                             waits=[("sw", pw_ld), ("od", o_all)]
                             + bank_w(bname), inc="pe")
                    e1 = E.op("vector", lambda e, t=t, half=half, pss=pss:
                         e.scalar_tensor_tensor(
                             out=aupd[:, half * 512:(half + 1) * 512],
                             in0=pss[:], scalar=1.0,
                             in1=bproj_b[:, half * 512:(half + 1) * 512],
                             op0=OP.mult, op1=OP.add),
                         waits=[("pe", c["pe"])], inc="dv")
                    bank_r(bname, "dv", e1)
                    e2 = E.op("vector", lambda e, t=t, half=half: e.tensor_tensor(
                        out=az[:, half * 512:(half + 1) * 512],
                        in0=aupd[:, half * 512:(half + 1) * 512],
                        in1=X[:, t, half * 512:(half + 1) * 512], op=OP.add),
                        waits=[("dv", e1)] + state["ln_src"][t], inc="dv")
                    xw.append(e2)
                e3 = E.op("vector", lambda e, t=t: e.tensor_copy(
                    out=X[:, t, :], in_=az[:]),
                    waits=[("dv", max(xw))], inc="dv")
                state["ln_src"][t] = [("dv", e3)]

        phases = getattr(cfg, "phases", "faf")
        def dump(name, src):
            if name in dbg:
                tgt = dbg[name]
                if name in ("dbg_x1", "dbg_x2"):
                    tgt = tgt.rearrange("(t p) d -> p t d", p=P)
                E.op("sync", lambda e, tgt=tgt, src=src: e.dma_start(
                    out=tgt, in_=src),
                    waits=[(k, v) for k, v in c.items() if v > 0], inc="st", n=16)

        emit_ffn(0)
        E.fence()
        dump("dbg_x1", X[:].rearrange("p t d -> t p d") if False else X[:])
        dump("dbg_eid", eid[:])
        dump("dbg_wgt", wgt[:])
        dump("dbg_xn", xn_tok[:, :, :])
        E.fence()
        if len(phases) > 1:
            emit_attn()
            E.fence()
            dump("dbg_x2", X[:])
            dump("dbg_zn", zn_fm[:, :, :])
            dump("dbg_o", O_fm[:, :, :])
            dump("dbg_q", qTp[:])
            dump("dbg_k", kTp[:])
            E.fence()
        if len(phases) > 2:
            emit_ffn(1)

        stw = [w for t in range(NT) for w in state["ln_src"][t]]
        E.op("sync", lambda e: e.dma_start(
            out=x_out.rearrange("(t p) d -> p t d", p=P), in_=X[:]),
            waits=stw, inc="st", n=16)
        E.op("sync", None, waits=[("st", c["st"])])
        E.flush()

    return nc



def _host_prep(inputs, cfg):
    D = cfg.D
    w_qkv = np.asarray(inputs["L0_attn_w_qkv"], dtype=np.float32)
    H, HD = 16, 64
    cols = np.arange(D)
    h, hd = cols // HD, cols % HD
    wq_a = np.ascontiguousarray(w_qkv[:, h * (HD * 3) + hd * 3 + 0])
    wk_a = np.ascontiguousarray(w_qkv[:, h * (HD * 3) + hd * 3 + 1])
    wv_a = np.ascontiguousarray(w_qkv[:, h * (HD * 3) + hd * 3 + 2])
    wqT = np.ascontiguousarray(np.asarray(inputs["L0_ffn_wq"], np.float32).T)
    keys = np.asarray(inputs["L0_ffn_keys"], np.float32)
    keysT = np.ascontiguousarray(
        keys.transpose(2, 0, 3, 1).reshape(cfg.PH, cfg.DK, cfg.NK))
    fused32 = np.ascontiguousarray(np.concatenate(
        [np.asarray(inputs["L0_ffn_table_down"], np.float32),
         np.asarray(inputs["L0_ffn_table_up"], np.float32)], axis=1))
    rowv = np.zeros((5, D), np.float32)
    rowv[0] = np.asarray(inputs["L0_ffn_gamma"], np.float32)
    rowv[1] = np.asarray(inputs["L0_attn_ln_scale"], np.float32)
    rowv[2] = np.asarray(inputs["L0_attn_ln_bias"], np.float32)
    rowv[3] = np.asarray(inputs["L0_attn_b_proj"], np.float32)
    rowv[4, 0:16] = np.arange(16, dtype=np.float32)
    ident = np.eye(P, dtype=np.float32)
    w_proj = np.ascontiguousarray(np.asarray(inputs["L0_attn_w_proj"], np.float32))
    shared = dict(fused32=fused32, wqT=wqT, keysT=keysT, wq_a=wq_a, wk_a=wk_a,
                  wv_a=wv_a, w_proj=w_proj, rowv=rowv, ident_in=ident)
    return shared


def kernel(**inputs):
    cfg = Cfg()
    z = np.asarray(inputs["L0_attn_z"], np.float32)
    ncore = z.shape[0]
    shared = _host_prep(inputs, cfg)
    in_maps = []
    for cidx in range(ncore):
        m = dict(shared)
        m["z_in"] = np.ascontiguousarray(z[cidx])
        in_maps.append(m)
    nc = build_nc(cfg)
    res = run_bass_kernel_spmd(nc, in_maps, core_ids=list(range(ncore)))
    out = np.stack([np.asarray(r["x_out"], np.float32) for r in res.results])
    return out
